# version 13
# baseline (speedup 1.0000x reference)
import numpy as np
import ml_dtypes
from contextlib import ExitStack
import concourse.bass as bass
import concourse.mybir as mybir
from concourse.bass_utils import run_bass_kernel_spmd

F32 = mybir.dt.float32
BF16 = mybir.dt.bfloat16
AF = mybir.ActivationFunctionType
ALU = mybir.AluOpType
AXX = mybir.AxisListType.X

D = 1024
NCORES = 8
TC = 2048
NT_X = 19
EXT = NT_X * 128
FXW = 2304
FD, SW, FA = 0, 896, 1600
EPS = 1e-5
WEPS = 1e-8

ST_SUM, ST_SQ, ST_MU0, ST_VAR, ST_A, ST_B, ST_S1, ST_S2, ST_S3 = (
    0, 19, 38, 57, 76, 95, 114, 133, 152)
PO = 171
STATS_W = 344
CP_BQ, CP_BKD, CP_BV, CP_BO, CP_BF, CP_W = 0, 8, 10, 11, 19, 27
CP_TMK, CP_TMV, CP_TMR, CP_MK1, CP_MV1, CP_MR1, CP_FM = 35, 43, 51, 59, 67, 75, 83
CP_EPS, CP_HALF = 84, 85
C2_LNG, C2_LNB, C2_MUR, C2_MUK, C2_MUR1, C2_MUK1, C2_KAP, C2_KIN = (
    0, 8, 16, 24, 32, 40, 48, 56)
CA_NIN, CA_DIN, CA_NF, CA_DF, CA_T0 = 0, 8, 16, 24, 32

WSCHED = [("wkv", 0), ("wq", 1), ("wo", 0), ("wq", 1), ("wq", 1),
          ("wf", 1), ("wa1", 2), ("wa2", 0), ("wf", 1), ("wa1", 2),
          ("wa2", 0), ("wf", 1), ("wa1", 2), ("wa2", 0),
          ("l1wk", 1), ("l1wv", 0), ("l1wr", 1), ("l1wo", 0),
          ("fwr", 1), ("fwk", 0), ("fwv", 1), ("fwr", 0), ("fwk", 1),
          ("fwv", 0), ("fwr", 1), ("fwk", 0), ("fwv", 1), ("fwr", 0),
          ("fwk", 1), ("fwv", 0)]
WCOLS = {"wkv": 384}
WNAMES = ["wkv", "wq", "wo", "wf", "wa1", "wa2",
          "l1wk", "l1wv", "l1wr", "l1wo", "fwr", "fwk", "fwv"]

Y_PASSES = [[1, 2, 3, 4, 5], [6, 7, 8, 9, 10], [11, 12, 13, 14, 15], [16, 17]]
MIX_BOUNDS = [(256, 768), (768, 1408), (1408, 2048), (2048, 2304)]


def build(stage=4, dbg=()):
    dbgnames = [n for n, _ in dbg]
    nc = bass.Bass("TRN2", target_bir_lowering=False, debug=False)
    stack = ExitStack()
    T, V, S, G, SY = nc.tensor, nc.vector, nc.scalar, nc.gpsimd, nc.sync

    NTP = 4
    sems, cnt = {}, {}
    for s_ in (["ldx", "ldc", "ldw", "mm", "ev", "vv", "pp", "cc", "sp", "st",
                "dbg"] + [f"tp{i}" for i in range(NTP)]):
        sems[s_] = stack.enter_context(nc.semaphore(s_))
        cnt[s_] = 0

    def inc(instr, name, n=1):
        instr.then_inc(sems[name], n)
        cnt[name] += n
        return cnt[name]

    def wait(eng, name, val=None):
        eng.wait_ge(sems[name], cnt[name] if val is None else val)

    def gdma(semname, lag=0, **kw):
        G.wait_ge(sems[semname], max(0, cnt[semname] - 16 * lag))
        return inc(G.dma_start(**kw), semname, 16)

    def sb(name, shape, dtype):
        return stack.enter_context(nc.sbuf_tensor(name, shape, dtype))

    def inp(name, shape, dtype=F32):
        return nc.dram_tensor(name, shape, dtype, kind="ExternalInput").ap()

    x0_h = inp("x0", [EXT, D])
    out_h = nc.dram_tensor("out", [TC, D], F32, kind="ExternalOutput").ap()
    whost = {n: inp(n, [D, WCOLS.get(n, 1024)], BF16) for n in WNAMES}
    cpack_h = inp("cpack", [128, 96])
    cpack2_h = inp("cpack2", [128, 64])
    kvmask_h = inp("kvmask", [128, 896], BF16)
    bmask_h = inp("bmask", [128, 2, 512], BF16)
    identbf_h = inp("identbf", [128, 128], BF16)
    rampbf_h = inp("rampbf", [128, TC], BF16)
    cc_in = nc.dram_tensor("cc_in", [128, 16], F32).ap()
    cc_out = nc.dram_tensor("cc_out", [NCORES * 128, 16], F32).ap()
    spill_h = nc.dram_tensor("spill", [8, 2, 128, TC], F32).ap()
    dbg_h = {n: nc.dram_tensor("dbg_" + n, s, F32, kind="ExternalOutput").ap()
             for n, s in dbg}

    stream = sb("stream", [128, 18, 1024], F32)
    fmX = sb("fmX", [128, 8, FXW], BF16)
    fmB = sb("fmB", [128, 8, 800], BF16)
    fmC = sb("fmC", [128, 8, 800], BF16)
    ptm = sb("ptm", [128, 2, 1024], BF16)
    trash = sb("trash", [128, 1024], BF16)
    wb = [sb(f"wb{i}", [128, 8, 1024], BF16) for i in range(2)]
    tmS = sb("tmS", [128, 2, 1024], BF16)
    tmF = sb("tmF", [128, 1024], F32)
    stats = sb("stats", [128, STATS_W], F32)
    cpk = sb("cpk", [128, 96], F32)
    cpk2 = sb("cpk2", [128, 64], F32)
    etd = sb("etd", [128, 8], F32)
    gath = sb("gath", [128, 8, 16], F32)
    carry = sb("carry", [128, 48], F32)
    ylagc = sb("ylagc", [128, 16], BF16)

    psA = stack.enter_context(nc.psum_tensor("psA", [128, 1536], F32))
    psB = stack.enter_context(nc.psum_tensor("psB", [128, 1536], F32))
    psC = stack.enter_context(nc.psum_tensor("psC", [128, 512], F32))
    psD = stack.enter_context(nc.psum_tensor("psD", [128, 512], F32))

    arena = nc.auto_arena()
    k2 = arena.sbuf_tensor("k2", [128, 2, EXT], BF16)
    v_tm = arena.sbuf_tensor("v_tm", [128, NT_X, 128], BF16)
    probs = arena.sbuf_tensor("probs", [128, 1536], BF16)
    rden = arena.sbuf_tensor("rden", [128, 512], F32)
    kvm = arena.sbuf_tensor("kvm", [128, 896], BF16)
    bmk = arena.sbuf_tensor("bmk", [128, 2, 512], BF16)
    ones_bf = arena.sbuf_tensor("ones_bf", [128, 128], BF16)
    ident_bf = arena.sbuf_tensor("ident_bf", [128, 128], BF16)

    arenaF = nc.auto_arena()
    wbF = arenaF.sbuf_tensor("wbF", [128, 8, 1024], BF16)

    arena2 = nc.auto_arena()
    scr = arena2.sbuf_tensor("scr", [128, 4, 656], F32)
    rampb = arena2.sbuf_tensor("rampb", [128, TC], BF16)
    tr2 = arena2.sbuf_tensor("tr2", [128, 8, 656], BF16)

    nc.const_aps.register(F32, EPS, cpk[:, CP_EPS:CP_EPS + 1])
    nc.const_aps.register(F32, 0.5, cpk[:, CP_HALF:CP_HALF + 1])

    wload_next = [0]
    winst = {}
    wdep = []
    occ = [None, None, None]
    for i, (nm, b) in enumerate(WSCHED):
        wdep.append(0 if occ[b] is None else occ[b] + 1)
        occ[b] = i
    wfree = {}

    wbF_first = [True]

    def load_weights_upto(sched_idx):
        while wload_next[0] <= sched_idx:
            i = wload_next[0]
            nm, b = WSCHED[i]
            ncols = WCOLS.get(nm, 1024)
            if wdep[i] > 0:
                prev = wdep[i] - 1
                assert prev in wfree, f"load {i} dep {prev} not yet consumed"
                G.wait_ge(sems["mm"], wfree[prev])
            if b == 2 and wbF_first[0]:
                wait(G, "mm"); wait(G, "ev"); wait(G, "vv")
                wbF_first[0] = False
            gdma("ldw", out=wb[b][:, :, 0:ncols],
                 in_=whost[nm].rearrange("(c p) m -> p c m", p=128))
            winst[nm] = i
            wload_next[0] += 1

    def wuse(name):
        i = winst[name]
        return i, wb[WSCHED[i][1]]

    def wait_weight(eng, name):
        eng.wait_ge(sems["ldw"], 16 * (winst[name] + 1))

    def weight_done(name):
        wfree[winst[name]] = cnt["mm"]

    banks = [psA[:, 0:512], psA[:, 512:1024], psA[:, 1024:1536],
             psB[:, 0:512], psB[:, 512:1024], psB[:, 1024:1536],
             psD[:, 0:512]]
    bmark = [None] * 7
    bptr = [0]

    def take_bank():
        bi = bptr[0] % 7
        bptr[0] += 1
        if bmark[bi] is not None:
            T.wait_ge(sems[bmark[bi][0]], bmark[bi][1])
        return bi, banks[bi]

    def nslices(cols, width=512):
        out, o = [], 0
        while o < cols:
            w = min(width, cols - o)
            out.append((o, w))
            o += w
        return out

    tile_tp = {}
    tpc = [0]
    tp_rr = [0]
    slotmark = [None, None]
    trash_war = [None]

    def tp_wait(eng, marks):
        byse = {}
        for s_, v in marks:
            byse[s_] = max(byse.get(s_, 0), v)
        for s_, v in byse.items():
            eng.wait_ge(sems[s_], v)

    def tpsnap():
        return ("tpsnap", tuple(cnt[f"tp{i}"] for i in range(NTP)))

    def war_wait(eng, war):
        if war is None:
            return
        if war[0] == "tpsnap":
            for i, v in enumerate(war[1]):
                if v:
                    eng.wait_ge(sems[f"tp{i}"], v)
        else:
            eng.wait_ge(sems[war[0]], war[1])

    def tp_dma(out, in_):
        s_ = f"tp{tp_rr[0] % NTP}"
        tp_rr[0] += 1
        SY.wait_ge(sems[s_], cnt[s_])
        it = SY.dma_start(out=out, in_=in_, transpose=True)
        return (s_, inc(it, s_, 16))

    def sweep_tile(src_ap, dst3, a_col, b_col, s_wait=None, sy_wait=None):
        slot = tpc[0] % 2
        if slotmark[slot] is not None:
            tp_wait(S, [slotmark[slot]])
        if s_wait is not None:
            for sv in (s_wait if isinstance(s_wait, list) else [s_wait]):
                war_wait(S, sv)
        ia = S.activation(tmS[:, slot, :], src_ap, AF.Identity,
                          scale=a_col, bias=b_col)
        ec = inc(ia, "ev", 1)
        wait(SY, "ev", ec)
        if sy_wait is not None:
            war_wait(SY, sy_wait)
        mark = tp_dma(dst3, tmS[:, slot, :])
        slotmark[slot] = mark
        tpc[0] += 1
        return mark

    def tile_wait(eng, tiles):
        tp_wait(eng, [tile_tp[t] for t in tiles])

    def ln_rows(n, base, reg=0, with_mu=True, keep_mu0=False):
        o = reg
        sl = lambda c: stats[:, o + c + base:o + c + base + n]
        wait(V, "ev"); wait(V, "vv"); wait(V, "pp")
        if with_mu:
            mu = stats[:, ST_MU0 + base:ST_MU0 + base + n] if keep_mu0 \
                else sl(ST_S3)
            V.tensor_scalar(mu, sl(ST_SUM), 1.0 / 1024.0, None, ALU.mult)
            V.drain()
        else:
            mu = stats[:, ST_MU0 + base:ST_MU0 + base + n]
        V.tensor_tensor(sl(ST_S2), mu, mu, ALU.mult)
        V.drain()
        V.scalar_tensor_tensor(sl(ST_VAR), sl(ST_SQ), 1.0 / 1024.0,
                               sl(ST_S2), ALU.mult, ALU.subtract)
        V.drain()
        vq = inc(V.nop(), "vv", 1)
        wait(S, "vv", vq)
        S.activation(sl(ST_S1), sl(ST_VAR), AF.Sqrt, bias=EPS)
        S.drain()
        eq = inc(S.nop(), "ev", 1)
        wait(V, "ev", eq)
        V.reciprocal(sl(ST_A), sl(ST_S1))
        V.drain()
        V.scalar_tensor_tensor(sl(ST_B), mu, -1.0, sl(ST_A),
                               ALU.mult, ALU.mult)
        V.drain()
        return inc(V.nop(), "vv", 1)

    def proj(wname, rhs_h, rhs_off, cols, mlist, evac, last_use=False,
             slice_wait=None, pre_wait=None, drain=False):
        wait_weight(T, wname)
        _, wbf_ = wuse(wname)
        out_marks = {}
        first = True
        for (n0, w) in nslices(cols):
            if slice_wait is not None:
                for sv in slice_wait(n0, w):
                    war_wait(T, sv)
            if first and pre_wait:
                for sv in pre_wait:
                    war_wait(T, sv)
            first = False
            for m in mlist:
                bi, ps = take_bank()
                last = None
                for k in range(8):
                    last = T.matmul(ps[:, 0:w],
                                    wbf_[:, k, 128 * m:128 * m + 128],
                                    rhs_h[:, k, rhs_off + n0:rhs_off + n0 + w],
                                    start=(k == 0), stop=(k == 7))
                mk = inc(last, "mm", 1)
                em = evac(m, n0, w, ps, mk)
                bmark[bi] = em
                out_marks[n0] = em
                if drain:
                    drain_thunks(1)
        if last_use:
            weight_done(wname)
        return out_marks

    ptm_war = [None, []]
    stream_mark = {}
    defq = []

    def drain_thunks(n=1):
        for _ in range(min(n, len(defq))):
            defq.pop(0)()

    def postln_group(tiles, src_h, src_off, ew, fold=True):
        ntl = len(tiles)
        war_wait(SY, ew)
        if ptm_war[0] is not None:
            for wv in ptm_war[0]:
                war_wait(SY, wv)
        gm = []
        for c in range(8):
            gm.append(tp_dma(ptm[:, 0:ntl, 128 * c:128 * c + 128],
                             src_h[:, c, src_off:src_off + 128 * ntl]))
        tp_wait(S, gm)
        for i in range(ntl):
            S.activation(ptm[:, i, :], ptm[:, i, :], AF.Identity,
                         accum_out=stats[:, PO + ST_SUM + i:PO + ST_SUM + i + 1])
            S.drain()
        se = inc(S.nop(), "ev", 1)
        tp_wait(V, gm)
        wait(V, "ev", se)
        if trash_war[0] is not None:
            tp_wait(V, [trash_war[0]])
        for i in range(ntl):
            V.scalar_tensor_tensor(trash[:, :], ptm[:, i, :], 1.0,
                                   ptm[:, i, :], ALU.mult, ALU.mult,
                                   accum_out=stats[:, PO + ST_SQ + i:
                                                   PO + ST_SQ + i + 1])
            V.drain()
        inc(V.nop(), "vv", 1)
        lnv = ln_rows(ntl, 0, reg=PO, with_mu=True)
        wait(V, "ev", se)
        for i, t in enumerate(tiles):
            wait(V, "vv", lnv)
            V.scalar_tensor_tensor(stream[:, t, :], ptm[:, i, :],
                                   stats[:, PO + ST_A + i:PO + ST_A + i + 1],
                                   stream[:, t, :], ALU.mult, ALU.add)
            V.drain()
            V.tensor_scalar(stream[:, t, :], stream[:, t, :],
                            stats[:, PO + ST_B + i:PO + ST_B + i + 1],
                            None, ALU.add)
            V.drain()
            if fold:
                V.scalar_tensor_tensor(trash[:, :], stream[:, t, :], 1.0,
                                       stream[:, t, :], ALU.mult, ALU.mult,
                                       accum_out=stats[:, ST_SQ + t:
                                                       ST_SQ + t + 1])
                V.drain()
            stream_mark[t] = ("vv", inc(V.nop(), "vv", 1))
        ptm_war[0] = [("ev", se), stream_mark[tiles[-1]]]

    def postln_pass(tiles, src_h, src_off_base, ew, fold=True):
        o = 0
        while o < len(tiles):
            grp = tiles[o:o + 2]
            postln_group(grp, src_h, src_off_base + 128 * o, ew, fold=fold)
            o += 2

    def postln_deferred(tiles_all, src_h, src_off_base, ew, fold=True,
                        eng="V", sweep_to=False, mini_ln=False, war_out=None):
        E = G if eng == "G" else V
        esem = "pp" if eng == "G" else "vv"
        gdst = trash[:, :] if eng == "V" else tmF[:, :]
        if war_out is not None:
            war_out[0] = []

        def mk_tp(o, grp):
            def f():
                ntl = len(grp)
                war_wait(SY, ew)
                if ptm_war[0] is not None:
                    for wv in ptm_war[0]:
                        war_wait(SY, wv)
                gm = []
                for c in range(8):
                    gm.append(tp_dma(
                        ptm[:, 0:ntl, 128 * c:128 * c + 128],
                        src_h[:, c, src_off_base + 128 * o:
                              src_off_base + 128 * o + 128 * ntl]))
                ptm_war[1] = gm
                if war_out is not None:
                    war_out[0].extend(gm)
            return f

        def mk_stats(i):
            def f():
                tp_wait(E, ptm_war[1])
                if eng == "V" and trash_war[0] is not None:
                    tp_wait(E, [trash_war[0]])
                E.tensor_scalar(gdst, ptm[:, i, :], 1.0, 0.0, ALU.mult,
                                ALU.add,
                                accum_out=stats[:, PO + ST_SUM + i:
                                                PO + ST_SUM + i + 1])
                E.drain()
                E.scalar_tensor_tensor(gdst, ptm[:, i, :], 1.0,
                                       ptm[:, i, :], ALU.mult, ALU.mult,
                                       accum_out=stats[:, PO + ST_SQ + i:
                                                       PO + ST_SQ + i + 1])
                E.drain()
                inc(E.nop(), esem, 1)
            return f

        def mk_ln(ntl):
            def f():
                lnv = ln_rows(ntl, 0, reg=PO, with_mu=True)
                ptm_war.append(lnv)
            return f

        def mk_apply(i, t, last):
            def f():
                lnv = ptm_war[-1]
                E.wait_ge(sems["vv"], lnv)
                E.scalar_tensor_tensor(stream[:, t, :], ptm[:, i, :],
                                       stats[:, PO + ST_A + i:
                                             PO + ST_A + i + 1],
                                       stream[:, t, :], ALU.mult, ALU.add)
                E.drain()
                E.tensor_scalar(stream[:, t, :], stream[:, t, :],
                                stats[:, PO + ST_B + i:PO + ST_B + i + 1],
                                0.0, ALU.add, ALU.add)
                E.drain()
                if fold:
                    E.scalar_tensor_tensor(gdst, stream[:, t, :], 1.0,
                                           stream[:, t, :], ALU.mult,
                                           ALU.mult,
                                           accum_out=stats[:, ST_SQ + t:
                                                           ST_SQ + t + 1])
                    E.drain()
                stream_mark[t] = (esem, inc(E.nop(), esem, 1))
                if last:
                    ptm_war[0] = [stream_mark[t]]
            return f

        for o in range(0, len(tiles_all), 2):
            grp = tiles_all[o:o + 2]
            defq.append(mk_tp(o, grp))
            for i in range(len(grp)):
                defq.append(mk_stats(i))
            defq.append(mk_ln(len(grp)))
            for i, t in enumerate(grp):
                defq.append(mk_apply(i, t, i == len(grp) - 1))

        if mini_ln:
            def f_mini2():
                stats_rdy[0] = ln_rows(len(tiles_all), tiles_all[0],
                                       with_mu=False)
            defq.append(f_mini2)

        if sweep_to:
            def f_mini():
                lnm = ln_rows(len(tiles_all), tiles_all[0], with_mu=False)
                ptm_war.append(("mini", lnm))
            defq.append(f_mini)

            def mk_sweep(t):
                def f():
                    lnm = ptm_war[-1][1]
                    tile_tp[t] = sweep_tile(
                        stream[:, t, :], fmX[:, :, 128 * t:128 * t + 128],
                        stats[:, ST_A + t:ST_A + t + 1],
                        stats[:, ST_B + t:ST_B + t + 1],
                        s_wait=[("vv", lnm), stream_mark[t]])
                return f
            for t in tiles_all:
                defq.append(mk_sweep(t))

    def dump(name, src):
        if name in dbgnames:
            drain_thunks(len(defq))
            wait(G, "vv"); wait(G, "ev"); wait(G, "mm"); wait(G, "pp")
            gdma("dbg", out=dbg_h[name], in_=src)
            wait(T, "dbg"); wait(S, "dbg"); wait(V, "dbg"); wait(SY, "dbg")

    wb.append(wbF)

    with nc.Block() as block:
        arena.allocate()

        gdma("ldx", out=stream[:, 0:9, :],
             in_=x0_h[0:1152, :].rearrange("(t p) d -> p t d", p=128))
        gdma("ldx", out=stream[:, 9:18, :],
             in_=x0_h[1152:2304, :].rearrange("(t p) d -> p t d", p=128))
        gdma("ldx", out=tmF[:, :], in_=x0_h[2304:2432, :])
        for dst, src in ((cpk[:, :], cpack_h), (cpk2[:, :], cpack2_h),
                         (kvm[:, :], kvmask_h), (bmk[:, :, :], bmask_h),
                         (ident_bf[:, :], identbf_h)):
            gdma("ldc", out=dst, in_=src)
        load_weights_upto(1)

        V.memset(ones_bf[:, :], 1.0)
        V.memset(carry[:, :], 0.0)
        V.drain()
        inc(V.nop(), "vv", 1)
        wait(S, "ldc")
        S.activation(etd[:, 0:8], cpk[:, CP_W:CP_W + 8], AF.Exp)
        S.drain()
        inc(S.nop(), "ev", 1)

        for t in range(19):
            lv = 16 if t < 9 else (32 if t < 18 else 48)
            src = stream[:, t, :] if t < 18 else tmF[:, :]
            wait(S, "ldx", lv)
            S.activation(tmS[:, 0, :], src, AF.Identity,
                         accum_out=stats[:, ST_SUM + t:ST_SUM + t + 1])
            S.drain()
            wait(V, "ldx", lv)
            V.scalar_tensor_tensor(trash[:, :], src, 1.0, src,
                                   ALU.mult, ALU.mult,
                                   accum_out=stats[:, ST_SQ + t:ST_SQ + t + 1])
            V.drain()
        inc(S.nop(), "ev", 1)
        inc(V.nop(), "vv", 1)
        ln1 = ln_rows(19, 0, with_mu=True, keep_mu0=True)

        for t in range(19):
            src = stream[:, t, :] if t < 18 else tmF[:, :]
            if t < 18:
                dst3 = fmX[:, :, 128 * t:128 * t + 128]
            else:
                dst3 = ptm[:, 0, :].rearrange("p (c q) -> p c q", q=128)
            tile_tp[t] = sweep_tile(src, dst3,
                                    stats[:, ST_A + t:ST_A + t + 1],
                                    stats[:, ST_B + t:ST_B + t + 1],
                                    s_wait=("vv", ln1))

        wait_weight(T, "wkv")
        _, wkvb = wuse("wkv")
        wait(V, "ldc")
        vtp_last = [None]
        k2_done = [0]
        kv_slices = [(0, 512), (512, 512), (1024, 512), (1536, 512),
                     (2048, 256), (2304, 128)]
        for si, (n0, w) in enumerate(kv_slices):
            if n0 < 2304:
                tile_wait(T, range(n0 // 128, (n0 + w - 1) // 128 + 1))
            else:
                tp_wait(T, [tile_tp[18]])
            edge = (si == 0 or si >= 4)
            if si == 0:
                kvs = kvm[:, 0:512]
            elif si == 4:
                kvs = kvm[:, 512:768]
            elif si == 5:
                kvs = kvm[:, 768:896]
            else:
                kvs = None
            for m in range(3):
                bi, ps = take_bank()
                last = None
                for k in range(8):
                    rr = (fmX[:, k, n0:n0 + w] if n0 < 2304
                          else ptm[:, 0, 128 * k:128 * k + 128])
                    last = T.matmul(ps[:, 0:w], wkvb[:, k, 128 * m:128 * m + 128],
                                    rr, start=(k == 0), stop=(k == 7))
                mk = inc(last, "mm", 1)
                wait(V, "mm", mk)
                if m < 2:
                    if edge:
                        V.scalar_tensor_tensor(k2[:, m, n0:n0 + w], ps[:, 0:w],
                                               cpk[:, CP_BKD + m:CP_BKD + m + 1],
                                               kvs, ALU.add, ALU.mult)
                    else:
                        V.tensor_scalar(k2[:, m, n0:n0 + w], ps[:, 0:w],
                                        cpk[:, CP_BKD + m:CP_BKD + m + 1],
                                        None, ALU.add)
                    V.drain()
                    em = ("vv", inc(V.nop(), "vv", 1))
                    k2_done[0] = em[1]
                else:
                    if vtp_last[0] is not None:
                        tp_wait(V, [vtp_last[0]])
                    if edge:
                        V.scalar_tensor_tensor(trash[:, 0:w], ps[:, 0:w],
                                               cpk[:, CP_BV:CP_BV + 1],
                                               kvs, ALU.add, ALU.mult)
                    else:
                        V.tensor_scalar(trash[:, 0:w], ps[:, 0:w],
                                        cpk[:, CP_BV:CP_BV + 1], None, ALU.add)
                    V.drain()
                    vv_ = inc(V.nop(), "vv", 1)
                    wait(SY, "vv", vv_)
                    vtp_last[0] = tp_dma(v_tm[:, n0 // 128:(n0 + w) // 128, :],
                                         trash[:, 0:w])
                    em = ("vv", vv_)
                bmark[bi] = em
        weight_done("wkv")
        trash_war[0] = vtp_last[0]
        load_weights_upto(2)
        dump("k2", k2[:, :, :])
        dump("stats0", stats[:, 0:133])
        if "vtm" in dbgnames:
            tp_wait(G, [vtp_last[0]])
            gdma("dbg", out=dbg_h["vtm"], in_=v_tm[:, :, :])
            wait(T, "dbg"); wait(S, "dbg"); wait(V, "dbg"); wait(SY, "dbg")

        q_passes = [list(range(1, 7)), list(range(7, 13)), list(range(13, 18))]
        wq_idx = [1, 3, 4]
        att = {"exp_free": [0, 0], "evac_free": 0, "recip_free": 0}
        for pi, tiles in enumerate(q_passes):
            ncols = 128 * len(tiles)
            poff = 128 * tiles[0]
            load_weights_upto(wq_idx[pi])
            lastp = (pi == len(q_passes) - 1)

            def evac_q(m, n0, w, ps, mk):
                wait(S, "mm", mk)
                S.activation(fmB[:, m, n0:n0 + w], ps[:, 0:w], AF.Identity,
                             bias=cpk[:, CP_BQ + m:CP_BQ + m + 1])
                S.drain()
                return ("ev", inc(S.nop(), "ev", 1))

            qm = proj("wq", fmX, poff, ncols, list(range(8)), evac_q,
                      last_use=True,
                      slice_wait=lambda n0, w: [
                          tile_tp[t] for t in
                          range((poff + n0) // 128,
                                (poff + n0 + w - 1) // 128 + 1)])
            if lastp and stage >= 2:
                load_weights_upto(5)
            if pi == 0:
                dump("q0", fmB[:, :, 0:768])

            wq_last = qm[max(qm)]
            T.wait_ge(sems["ev"], wq_last[1])
            T.wait_ge(sems["vv"], k2_done[0])
            wait(T, "ldc")
            tp_wait(T, [vtp_last[0]])
            for bi in range(7):
                if bmark[bi] is not None:
                    T.wait_ge(sems[bmark[bi][0]], bmark[bi][1])
                    bmark[bi] = None
            kc_map = [(0, -1, 0), (1, 1, 1), (2, 0, None)]

            def att_scores(j, g, qoff, par):
                pssc = psA if par == 0 else psB
                wait(T, "ev", att["exp_free"][par])
                sc_last = None
                for (r, dj, msk) in kc_map:
                    kt = j + dj
                    sc_last = T.matmul(
                        pssc[:, 512 * r:512 * r + 512],
                        k2[64 * par:64 * par + 64, g,
                           128 * kt:128 * kt + 128],
                        fmB[64 * par:64 * par + 64, 4 * g:4 * g + 4,
                            qoff:qoff + 128],
                        start=True, stop=(msk is None))
                    if msk is not None:
                        sc_last = T.matmul(
                            pssc[:, 512 * r:512 * r + 512],
                            ident_bf[:, :], bmk[:, msk, :],
                            start=False, stop=True)
                return inc(sc_last, "mm", 1)

            def att_exp(par, mk, probs_war):
                pssc = psA if par == 0 else psB
                wait(S, "mm", max(mk, probs_war))
                S.activation(probs[:, :], pssc[:, :], AF.Exp)
                S.drain()
                emk = inc(S.nop(), "ev", 1)
                att["exp_free"][par] = emk
                return emk

            def att_denav(j, g, par, emk):
                wait(T, "ev", emk)
                if par == 0:
                    wait(T, "vv", att["recip_free"])
                    wait(T, "vv", att["evac_free"])
                dlast = None
                for (r, dj, msk) in kc_map:
                    dlast = T.matmul(
                        psC[64 * par:64 * par + 64, 0:512],
                        ones_bf[:, 0:64],
                        probs[:, 512 * r:512 * r + 512],
                        start=(r == 0), stop=(r == 2),
                        tile_position=(0, 64 * par))
                inc(dlast, "mm", 1)
                avlast = None
                for (r, dj, msk) in kc_map:
                    kt = j + dj
                    avlast = T.matmul(
                        psD[64 * par:64 * par + 64, 0:512],
                        v_tm[:, kt, 64 * g:64 * g + 64],
                        probs[:, 512 * r:512 * r + 512],
                        start=(r == 0), stop=(r == 2),
                        tile_position=(0, 64 * par))
                return inc(avlast, "mm", 1)

            probs_war = [0]
            for ji, j in enumerate(tiles):
                qoff = 128 * ji
                for g in range(2):
                    mk0 = att_scores(j, g, qoff, 0)
                    emk0 = att_exp(0, mk0, probs_war[0])
                    mk1 = att_scores(j, g, qoff, 1)
                    amk0 = att_denav(j, g, 0, emk0)
                    emk1 = att_exp(1, mk1, amk0)
                    amk1 = att_denav(j, g, 1, emk1)
                    probs_war[0] = amk1
                    wait(V, "mm", amk1)
                    V.reciprocal(rden[:, :], psC[:, 0:512])
                    V.drain()
                    att["recip_free"] = inc(V.nop(), "vv", 1)
                    V.tensor_tensor(
                        fmC[:, 4 * g:4 * g + 4, qoff:qoff + 128],
                        psD[:, 0:512].rearrange("p (i q) -> p i q", q=128),
                        rden[:, :].rearrange("p (i q) -> p i q", q=128),
                        ALU.mult)
                    V.drain()
                    att["evac_free"] = inc(V.nop(), "vv", 1)
                    drain_thunks(1)
            if pi == 0:
                dump("at0", fmC[:, :, 0:768])
            for bi in range(3):
                bmark[bi] = ("ev", att["exp_free"][0])
            for bi in range(3, 6):
                bmark[bi] = ("ev", att["exp_free"][1])
            bmark[6] = ("vv", att["evac_free"])

            def evac_p(m, n0, w, ps, mk, poff=poff):
                wait(S, "mm", mk)
                S.activation(fmX[:, m, poff + n0:poff + n0 + w], ps[:, 0:w],
                             AF.Identity, bias=cpk[:, CP_BO + m:CP_BO + m + 1])
                S.drain()
                return ("ev", inc(S.nop(), "ev", 1))

            wm = proj("wo", fmC, 0, ncols, list(range(8)), evac_p,
                      last_use=lastp, drain=True,
                      pre_wait=[("vv", att["evac_free"])])
            wo_last = wm[max(wm)]
            postln_deferred(tiles, fmX, poff, wo_last, fold=True, eng="V",
                            sweep_to=(stage >= 2))

        arena.free()
        arenaF.allocate()
        if stage >= 2:
            load_weights_upto(6)
        dump("x1", stream[:, :, :])

        if stage >= 2:
            f_passes = [list(range(1, 7)), list(range(7, 13)),
                        list(range(13, 18))]
            for pi, tiles in enumerate(f_passes):
                ncols = 128 * len(tiles)
                poff = 128 * tiles[0]
                load_weights_upto(5 + 3 * pi)

                def evac_h(dst):
                    def e(m, n0, w, ps, mk):
                        wait(S, "mm", mk)
                        S.activation(dst[:, m, n0:n0 + w], ps[:, 0:w],
                                     AF.Identity)
                        S.drain()
                        return ("ev", inc(S.nop(), "ev", 1))
                    return e

                m1 = proj("wf", fmX, poff, ncols, list(range(8)),
                          evac_h(fmB), last_use=True, drain=True,
                          slice_wait=lambda n0, w: [
                              tile_tp[t] for t in
                              range((poff + n0) // 128,
                                    (poff + n0 + w - 1) // 128 + 1)])
                load_weights_upto(6 + 3 * pi)
                m2 = proj("wa1", fmB, 0, ncols, list(range(8)),
                          evac_h(fmC), last_use=True, drain=True,
                          slice_wait=lambda n0, w: [m1[n0]])
                load_weights_upto(7 + 3 * pi)

                def evac_tanh(m, n0, w, ps, mk, poff=poff):
                    wait(S, "mm", mk)
                    S.activation(fmX[:, m, poff + n0:poff + n0 + w], ps[:, 0:w],
                                 AF.Tanh, bias=cpk[:, CP_BF + m:CP_BF + m + 1])
                    S.drain()
                    return ("ev", inc(S.nop(), "ev", 1))

                m3 = proj("wa2", fmC, 0, ncols, list(range(8)), evac_tanh,
                          last_use=True, drain=True,
                          slice_wait=lambda n0, w: [m2[n0]])
                postln_deferred(tiles, fmX, poff, m3[max(m3)], fold=True,
                                eng="V")
            drain_thunks(len(defq))

            dump("x2", stream[:, :, :])

        fd_war = [None]
        fa_war = [None]
        fc_war = [None]
        fb_war = [[]]
        tr2_war = [0]

        def emit_y_pass(tiles, gcol=None, bcol=None, mask255=False,
                        restore_lag=False):
            Wp = 128 * len(tiles)
            for i, t in enumerate(tiles):
                tile_tp[t] = sweep_tile(
                    stream[:, t, :],
                    fmX[:, :, FD + 128 * (i + 1):FD + 128 * (i + 2)],
                    stats[:, ST_A + t:ST_A + t + 1],
                    stats[:, ST_B + t:ST_B + t + 1],
                    s_wait=[("vv", stats_rdy[0]),
                            stream_mark.get(t, ("vv", 0))],
                    sy_wait=(fd_war[0] if i == 0 else None))
            tp_wait(V, [tile_tp[t] for t in tiles])
            if gcol is not None:
                for c in range(8):
                    V.tensor_scalar(fmX[:, c, FD + 128:FD + 128 + Wp],
                                    fmX[:, c, FD + 128:FD + 128 + Wp],
                                    cpk2[:, gcol + c:gcol + c + 1],
                                    cpk2[:, bcol + c:bcol + c + 1],
                                    ALU.mult, ALU.add)
                V.drain()
            if mask255:
                i255 = FD + 255 - 128 * tiles[0] + 128
                V.tensor_scalar(fmX[:, :, i255:i255 + 1],
                                fmX[:, :, i255:i255 + 1],
                                cpk[:, CP_FM:CP_FM + 1], None, ALU.mult)
                V.drain()
            if restore_lag:
                V.tensor_copy(fmX[:, :, FD + 127:FD + 128],
                              ylagc[:, 0:8].rearrange("p (c one) -> p c one",
                                                      one=1))
                V.drain()
            V.tensor_copy(ylagc[:, 0:8].rearrange("p (c one) -> p c one",
                                                  one=1),
                          fmX[:, :, FD + 128 + Wp - 1:FD + 128 + Wp])
            V.drain()
            return inc(V.nop(), "vv", 1)

        def emit_mix(dst_off, tiles, b0, b1, ctensor, tm_c, tm1_c, war):
            yb = 128 * tiles[0] - 128
            i0, i1 = b0 - yb, b1 - yb
            wait(V, "vv")
            war_wait(V, war[0])
            for c in range(8):
                V.tensor_scalar(fmX[:, c, dst_off:dst_off + i1 - i0],
                                fmX[:, c, FD + i0:FD + i1],
                                ctensor[:, tm_c + c:tm_c + c + 1], None,
                                ALU.mult)
            V.drain()
            for c in range(8):
                V.scalar_tensor_tensor(fmX[:, c, dst_off:dst_off + i1 - i0],
                                       fmX[:, c, FD + i0 - 1:FD + i1 - 1],
                                       ctensor[:, tm1_c + c:tm1_c + c + 1],
                                       fmX[:, c, dst_off:dst_off + i1 - i0],
                                       ALU.mult, ALU.add)
            V.drain()
            return inc(V.nop(), "vv", 1)

        def emit_mix_fc(tiles, b0, b1, tm_c, tm1_c):
            yb = 128 * tiles[0] - 128
            i0, i1 = b0 - yb, b1 - yb
            wait(V, "vv")
            war_wait(V, fc_war[0])
            for c in range(8):
                V.tensor_scalar(fmC[:, c, 0:i1 - i0],
                                fmX[:, c, FD + i0:FD + i1],
                                cpk[:, tm_c + c:tm_c + c + 1], None, ALU.mult)
            V.drain()
            for c in range(8):
                V.scalar_tensor_tensor(fmC[:, c, 0:i1 - i0],
                                       fmX[:, c, FD + i0 - 1:FD + i1 - 1],
                                       cpk[:, tm1_c + c:tm1_c + c + 1],
                                       fmC[:, c, 0:i1 - i0],
                                       ALU.mult, ALU.add)
            V.drain()
            return inc(V.nop(), "vv", 1)

        stats_rdy = [0]

        if stage >= 3:
            wait(G, "vv"); wait(G, "ev"); wait(G, "mm")
            arenaF.free()
            arena2.allocate()
            gdma("ldc", out=rampb[:, :], in_=rampbf_h)
            load_weights_upto(15)
            fa_war[0] = tpsnap()
            fd_war[0] = tpsnap()
            fc_war[0] = ("mm", cnt["mm"])

            n, base = 17, 1
            sl = lambda c: stats[:, c + base:c + base + n]
            wait(V, "vv"); wait(V, "ev")
            mu0s = stats[:, ST_MU0 + base:ST_MU0 + base + n]
            V.tensor_tensor(sl(ST_S2), mu0s, mu0s, ALU.mult)
            V.drain()
            V.scalar_tensor_tensor(sl(ST_VAR), sl(ST_SQ), 1.0 / 1024.0,
                                   sl(ST_S2), ALU.mult, ALU.subtract)
            V.drain()
            vq = inc(V.nop(), "vv", 1)
            wait(S, "vv", vq)
            S.activation(sl(ST_S1), sl(ST_VAR), AF.Sqrt, bias=EPS)
            S.drain()
            eq = inc(S.nop(), "ev", 1)
            wait(V, "ev", eq)
            V.reciprocal(sl(ST_A), sl(ST_S1))
            V.drain()
            V.tensor_tensor(sl(ST_S2), sl(ST_A), sl(ST_A), ALU.mult)
            V.drain()
            V.tensor_tensor(sl(ST_S2), sl(ST_S2), sl(ST_VAR), ALU.mult)
            V.drain()
            vq = inc(V.nop(), "vv", 1)
            wait(S, "vv", vq)
            S.activation(sl(ST_S1), sl(ST_S2), AF.Sqrt, bias=EPS)
            S.drain()
            eq = inc(S.nop(), "ev", 1)
            wait(V, "ev", eq)
            V.reciprocal(sl(ST_S3), sl(ST_S1))
            V.drain()
            V.tensor_tensor(sl(ST_A), sl(ST_A), sl(ST_S3), ALU.mult)
            V.drain()
            V.scalar_tensor_tensor(sl(ST_B), mu0s, -1.0, sl(ST_A),
                                   ALU.mult, ALU.mult)
            V.drain()
            stats_rdy[0] = inc(V.nop(), "vv", 1)

            spill_mark = [0, 0]
            for pi, tiles in enumerate(Y_PASSES):
                b0, b1 = MIX_BOUNDS[pi]
                ncols = b1 - b0
                lastp = (pi == len(Y_PASSES) - 1)
                ym = emit_y_pass(tiles, gcol=C2_LNG, bcol=C2_LNB,
                                 mask255=(pi == 0), restore_lag=(pi > 0))
                km = emit_mix(FA, tiles, b0, b1, cpk, CP_TMK, CP_MK1, fa_war)
                vm = emit_mix_fc(tiles, b0, b1, CP_TMV, CP_MV1)
                fd_war[0] = ("vv", vm)
                wait(T, "vv", vm)
                wait_weight(T, "l1wk"); wait_weight(T, "l1wv")
                _, wkb = wuse("l1wk")
                _, wvb = wuse("l1wv")
                kv_mm = 0
                for c in range(8):
                    sa, sb_ = (0, 1) if c % 2 == 0 else (2, 3)
                    klast = None
                    for (n0, w) in nslices(ncols):
                        bi, ps = take_bank()
                        for k in range(8):
                            klast = T.matmul(ps[:, 0:w],
                                             wkb[:, k, 128 * c:128 * c + 128],
                                             fmX[:, k, FA + n0:FA + n0 + w],
                                             start=(k == 0), stop=(k == 7))
                        mk = inc(klast, "mm", 1)
                        wait(S, "mm", mk)
                        S.wait_ge(sems["sp"], spill_mark[c % 2])
                        S.activation(scr[:, sa, n0:n0 + w], ps[:, 0:w], AF.Exp)
                        S.drain()
                        bmark[bi] = ("ev", inc(S.nop(), "ev", 1))
                    vlast = None
                    for (n0, w) in nslices(ncols):
                        bi, ps = take_bank()
                        for k in range(8):
                            vlast = T.matmul(ps[:, 0:w],
                                             wvb[:, k, 128 * c:128 * c + 128],
                                             fmC[:, k, n0:n0 + w],
                                             start=(k == 0), stop=(k == 7))
                        mk = inc(vlast, "mm", 1)
                        wait(S, "mm", mk)
                        S.activation(scr[:, sb_, n0:n0 + w], ps[:, 0:w],
                                     AF.Identity)
                        S.drain()
                        bmark[bi] = ("ev", inc(S.nop(), "ev", 1))
                    kv_mm = cnt["mm"]
                    if lastp and c == 7:
                        weight_done("l1wk")
                        weight_done("l1wv")
                    wait(V, "ev")
                    V.tensor_tensor(scr[:, sb_, 0:ncols], scr[:, sa, 0:ncols],
                                    scr[:, sb_, 0:ncols], ALU.mult)
                    V.drain()
                    V.tensor_tensor_scan(scr[:, sb_, 0:ncols],
                                         etd[:, c:c + 1].broadcast_to(
                                             (128, ncols)),
                                         scr[:, sb_, 0:ncols],
                                         carry[:, CA_NF + c:CA_NF + c + 1],
                                         ALU.mult, ALU.add)
                    V.drain()
                    V.tensor_tensor_scan(scr[:, sa, 0:ncols],
                                         etd[:, c:c + 1].broadcast_to(
                                             (128, ncols)),
                                         scr[:, sa, 0:ncols],
                                         carry[:, CA_DF + c:CA_DF + c + 1],
                                         ALU.mult, ALU.add)
                    V.drain()
                    V.tensor_copy(carry[:, CA_NF + c:CA_NF + c + 1],
                                  scr[:, sb_, ncols - 1:ncols])
                    V.tensor_copy(carry[:, CA_DF + c:CA_DF + c + 1],
                                  scr[:, sa, ncols - 1:ncols])
                    V.drain()
                    svk = inc(V.nop(), "vv", 1)
                    G.wait_ge(sems["vv"], svk)
                    o0 = b0 - 256
                    gdma("sp", out=spill_h[c, 0, :, o0:o0 + ncols],
                         in_=scr[:, sb_, 0:ncols])
                    spill_mark[c % 2] = gdma(
                        "sp", out=spill_h[c, 1, :, o0:o0 + ncols],
                        in_=scr[:, sa, 0:ncols])
                fa_war[0] = ("mm", kv_mm)
                fc_war[0] = ("mm", kv_mm)

            wait(G, "vv")
            gdma("sp", out=cc_in, in_=carry[:, CA_NF:CA_NF + 16])
            wait(G, "sp")
            ic = G.collective_compute(
                "AllGather", ALU.bypass,
                replica_groups=[list(range(NCORES))],
                ins=[cc_in.opt()], outs=[cc_out.opt()])
            inc(ic, "cc", 1)
            wait(G, "cc")
            gk = gdma("sp", out=gath[:, :, :],
                      in_=cc_out.rearrange("(r p) k -> p r k", p=128))
            load_weights_upto(17)

            def phaseB_head(pi, tiles, b0, b1):
                ym = emit_y_pass(tiles, gcol=C2_LNG, bcol=C2_LNB,
                                 mask255=(pi == 0), restore_lag=(pi > 0))
                rb0 = 255 if pi == 0 else b0
                rm = emit_mix(FA, tiles, rb0, b1, cpk, CP_TMR, CP_MR1, fa_war)
                fd_war[0] = ("vv", rm)
                return rm, rb0

            rm0, rb0_0 = phaseB_head(0, Y_PASSES[0], *MIX_BOUNDS[0])
            V.memset(fmX[:, :, SW:SW + 704], 0.0)
            V.drain()
            srw_init = inc(V.nop(), "vv", 1)

            V.wait_ge(sems["sp"], gk)
            wait(V, "ldc")
            for c in range(8):
                V.tensor_scalar(stats[:, PO + 8 * c:PO + 8 * c + 8],
                                cpk2[:, C2_KAP:C2_KAP + 8],
                                cpk[:, CP_W + c:CP_W + c + 1], None, ALU.mult)
            V.drain()
            vq = inc(V.nop(), "vv", 1)
            wait(S, "vv", vq)
            S.activation(stats[:, PO:PO + 64], stats[:, PO:PO + 64], AF.Exp)
            S.drain()
            eq = inc(S.nop(), "ev", 1)
            wait(V, "ev", eq)
            for c in range(8):
                sl8 = stats[:, PO + 8 * c:PO + 8 * c + 8]
                V.tensor_tensor(sl8, sl8, cpk2[:, C2_KIN:C2_KIN + 8],
                                ALU.mult)
                V.drain()
                V.tensor_tensor(stats[:, ST_S3:ST_S3 + 8], sl8,
                                gath[:, :, c], ALU.mult)
                V.drain()
                V.reduce_sum(carry[:, CA_NIN + c:CA_NIN + c + 1],
                             stats[:, ST_S3:ST_S3 + 8], AXX)
                V.drain()
                V.tensor_tensor(stats[:, ST_S3:ST_S3 + 8], sl8,
                                gath[:, :, 8 + c], ALU.mult)
                V.drain()
                V.reduce_sum(carry[:, CA_DIN + c:CA_DIN + c + 1],
                             stats[:, ST_S3:ST_S3 + 8], AXX)
                V.drain()
            V.tensor_scalar(carry[:, CA_T0:CA_T0 + 8],
                            carry[:, CA_DIN:CA_DIN + 8], WEPS, None, ALU.add)
            V.drain()
            V.reciprocal(carry[:, CA_T0:CA_T0 + 8], carry[:, CA_T0:CA_T0 + 8])
            V.drain()
            V.tensor_tensor(carry[:, CA_T0:CA_T0 + 8],
                            carry[:, CA_T0:CA_T0 + 8],
                            carry[:, CA_NIN:CA_NIN + 8], ALU.mult)
            V.drain()
            inc(V.nop(), "vv", 1)

            wait_weight(V, "l1wo")
            _, wob = wuse("l1wo")
            wait(V, "mm")
            V.tensor_scalar(wob[:, :, :], wob[:, :, :], 0.5, None, ALU.mult)
            V.drain()
            wo_scaled = inc(V.nop(), "vv", 1)

            reload_mark = [0]
            rm_cur, rb0_cur = rm0, rb0_0
            head_next = [None]
            for pi, tiles in enumerate(Y_PASSES):
                b0, b1 = MIX_BOUNDS[pi]
                rm, rb0 = rm_cur, rb0_cur
                yb = 128 * tiles[0]
                wt = tiles
                wt_base = 128 * wt[0]
                lastp = (pi == len(Y_PASSES) - 1)
                wait(T, "vv", rm)
                wait_weight(T, "l1wr")
                _, wrb = wuse("l1wr")
                rcols = b1 - rb0
                r_mm = 0
                c_mark = cnt["vv"]
                for c in range(8):
                    drain_thunks(1)
                    wait(S, "vv", c_mark)
                    rlast = None
                    for (n0, w) in nslices(rcols):
                        bi, ps = take_bank()
                        for k in range(8):
                            rlast = T.matmul(ps[:, 0:w],
                                             wrb[:, k, 128 * c:128 * c + 128],
                                             fmX[:, k, FA + n0:FA + n0 + w],
                                             start=(k == 0), stop=(k == 7))
                        mk = inc(rlast, "mm", 1)
                        wait(S, "mm", mk)
                        S.activation(scr[:, 3, n0:n0 + w], ps[:, 0:w], AF.Tanh,
                                     scale=0.5)
                        S.drain()
                        bmark[bi] = ("ev", inc(S.nop(), "ev", 1))
                    r_mm = cnt["mm"]
                    if lastp and c == 7:
                        weight_done("l1wr")
                    o0 = b0 - 256
                    ocols = b1 - b0
                    G.wait_ge(sems["vv"], c_mark)
                    gdma("sp", out=scr[:, 0, 0:ocols],
                         in_=spill_h[c, 0, :, o0:o0 + ocols])
                    rlk = gdma("sp", out=scr[:, 1, 0:ocols],
                               in_=spill_h[c, 1, :, o0:o0 + ocols])
                    S.wait_ge(sems["sp"], rlk)
                    S.wait_ge(sems["vv"], c_mark)
                    S.activation(scr[:, 2, 0:ocols], rampb[:, o0:o0 + ocols],
                                 AF.Exp, scale=cpk[:, CP_W + c:CP_W + c + 1])
                    S.drain()
                    dk = inc(S.nop(), "ev", 1)
                    wait(V, "ev", dk)
                    V.scalar_tensor_tensor(scr[:, 0, 0:ocols],
                                           scr[:, 2, 0:ocols],
                                           carry[:, CA_NIN + c:CA_NIN + c + 1],
                                           scr[:, 0, 0:ocols],
                                           ALU.mult, ALU.add)
                    V.drain()
                    V.scalar_tensor_tensor(scr[:, 1, 0:ocols],
                                           scr[:, 2, 0:ocols],
                                           carry[:, CA_DIN + c:CA_DIN + c + 1],
                                           scr[:, 1, 0:ocols],
                                           ALU.mult, ALU.add)
                    V.drain()
                    V.tensor_scalar(scr[:, 1, 0:ocols], scr[:, 1, 0:ocols],
                                    WEPS, None, ALU.add)
                    V.drain()
                    V.reciprocal(scr[:, 1, 0:ocols], scr[:, 1, 0:ocols])
                    V.drain()
                    V.tensor_tensor(scr[:, 2, 0:ocols], scr[:, 0, 0:ocols],
                                    scr[:, 1, 0:ocols], ALU.mult)
                    V.drain()
                    toff = b0 - rb0
                    if pi == 0:
                        V.wait_ge(sems["vv"], srw_init)
                    V.scalar_tensor_tensor(
                        fmX[:, c, SW + (b0 - wt_base):
                            SW + (b0 - wt_base) + ocols],
                        scr[:, 3, toff:toff + ocols], 1.0,
                        scr[:, 2, 0:ocols], ALU.add, ALU.mult)
                    V.drain()
                    if pi == 0:
                        V.scalar_tensor_tensor(
                            fmX[:, c, SW + 127:SW + 128],
                            scr[:, 3, 0:1], 1.0,
                            carry[:, CA_T0 + c:CA_T0 + c + 1],
                            ALU.add, ALU.mult)
                        V.drain()
                    c_mark = inc(V.nop(), "vv", 1)
                fa_war[0] = ("mm", r_mm)
                srw_done = c_mark
                if not lastp:
                    rm_cur, rb0_cur = phaseB_head(pi + 1, Y_PASSES[pi + 1],
                                                  *MIX_BOUNDS[pi + 1])
                drain_thunks(len(defq))
                wait(T, "vv", max(srw_done, wo_scaled))

                def evac_po(m, n0, w, ps, mk):
                    wait(S, "mm", mk)
                    for wv in fb_war[0]:
                        war_wait(S, wv)
                    S.activation(fmB[:, m, n0:n0 + w], ps[:, 0:w], AF.Identity)
                    S.drain()
                    return ("ev", inc(S.nop(), "ev", 1))

                om = proj("l1wo", fmX, SW, 128 * len(wt), list(range(8)),
                          evac_po, last_use=lastp, drain=True)
                postln_deferred(wt, fmB, 0, om[max(om)], fold=True, eng="V",
                                mini_ln=True, war_out=fb_war)

            dump("x3", stream[:, :, :])

        if stage >= 4:
            wait(G, "vv"); wait(G, "ev"); wait(G, "mm")
            relu_sc = [scr[:, 2, 0:512], scr[:, 3, 0:512]]
            sq_mark = [cnt["vv"], cnt["vv"]]
            sw_war = [("mm", cnt["mm"])]
            f1_post = [[2, 3, 4, 5], [6, 7, 8, 9, 10],
                       [11, 12, 13, 14, 15], [16, 17]]
            load_weights_upto(18)
            emit_y_pass(Y_PASSES[0], mask255=True, restore_lag=False)
            rm_c = emit_mix(FA, Y_PASSES[0], *MIX_BOUNDS[0], cpk2,
                            C2_MUR, C2_MUR1, fa_war)
            km_c = emit_mix(SW, Y_PASSES[0], *MIX_BOUNDS[0], cpk2,
                            C2_MUK, C2_MUK1, sw_war)
            fd_war[0] = ("vv", km_c)
            for pi, tiles in enumerate(Y_PASSES):
                b0, b1 = MIX_BOUNDS[pi]
                ncols = b1 - b0
                lastp = (pi == len(Y_PASSES) - 1)
                load_weights_upto(18 + 3 * pi)
                rm, km = rm_c, km_c
                wait(T, "vv", rm)

                def evac_r2(m, n0, w, ps, mk):
                    wait(S, "mm", mk)
                    S.wait_ge(sems["vv"], tr2_war[0])
                    S.activation(tr2[:, m, n0:n0 + w], ps[:, 0:w], AF.Tanh,
                                 scale=0.5)
                    S.drain()
                    return ("ev", inc(S.nop(), "ev", 1))

                proj("fwr", fmX, FA, ncols, list(range(8)), evac_r2,
                     last_use=True, drain=True)
                fa_war[0] = ("mm", cnt["mm"])
                load_weights_upto(19 + 3 * pi)
                load_weights_upto(20 + 3 * pi)
                wait_weight(V, "fwv")
                _, wvb2 = wuse("fwv")
                wait(V, "mm")
                V.tensor_scalar(wvb2[:, :, :], wvb2[:, :, :], 0.5, None,
                                ALU.mult)
                V.drain()
                fwv_scaled = inc(V.nop(), "vv", 1)
                wait(T, "vv", km)

                ecnt = [0]

                def evac_relu_sq(m, n0, w, ps, mk):
                    par = ecnt[0] % 2
                    ecnt[0] += 1
                    wait(S, "mm", mk)
                    S.wait_ge(sems["vv"], sq_mark[par])
                    S.activation(relu_sc[par][:, 0:w], ps[:, 0:w], AF.Relu)
                    S.drain()
                    ek = inc(S.nop(), "ev", 1)
                    wait(V, "ev", ek)
                    V.tensor_tensor(fmC[:, m, n0:n0 + w],
                                    relu_sc[par][:, 0:w],
                                    relu_sc[par][:, 0:w], ALU.mult)
                    V.drain()
                    sq_mark[par] = inc(V.nop(), "vv", 1)
                    return ("vv", sq_mark[par])

                kpm = proj("fwk", fmX, SW, ncols, list(range(8)),
                           evac_relu_sq, last_use=True, drain=True)
                sw_war[0] = ("mm", cnt["mm"])
                kv_done = cnt["vv"]
                if not lastp:
                    emit_y_pass(Y_PASSES[pi + 1], mask255=False,
                                restore_lag=True)
                    rm_c = emit_mix(FA, Y_PASSES[pi + 1], *MIX_BOUNDS[pi + 1],
                                    cpk2, C2_MUR, C2_MUR1, fa_war)
                    km_c = emit_mix(SW, Y_PASSES[pi + 1], *MIX_BOUNDS[pi + 1],
                                    cpk2, C2_MUK, C2_MUK1, sw_war)
                    fd_war[0] = ("vv", km_c)
                wait(T, "vv", max(kv_done, fwv_scaled))

                drain_thunks(len(defq))

                def evac_out(m, n0, w, ps, mk):
                    wait(V, "mm", mk)
                    for wv in fb_war[0]:
                        war_wait(V, wv)
                    V.scalar_tensor_tensor(fmB[:, m, n0:n0 + w],
                                           tr2[:, m, n0:n0 + w], 1.0,
                                           ps[:, 0:w], ALU.add, ALU.mult)
                    V.drain()
                    return ("vv", inc(V.nop(), "vv", 1))

                vm_ = proj("fwv", fmC, 0, ncols, list(range(8)), evac_out,
                           last_use=True,
                           slice_wait=lambda n0, w: [kpm[n0]])
                tr2_war[0] = vm_[max(vm_)][1]
                pt = f1_post[pi]
                postln_deferred(pt, fmB, 128 * pt[0] - b0,
                                vm_[max(vm_)], fold=False, eng="V",
                                war_out=fb_war)

        drain_thunks(len(defq))
        wait(SY, "vv"); wait(SY, "ev"); wait(SY, "pp")
        iout = SY.dma_start(out=out_h.rearrange("(t p) d -> p t d", p=128),
                            in_=stream[:, 2:18, :])
        inc(iout, "st", 16)
        wait(SY, "st")
        if stage >= 3:
            arena2.free()
        else:
            arenaF.free()

    stack.close()
    return nc, dbg_h


def _fm_vec(v):
    return np.asarray(v, np.float32).reshape(8, 128).T.copy()


def _prep_core_inputs(core, full):
    b, s = core // 4, core % 4
    T0 = 2048 * s
    f32 = np.float32
    bf16 = ml_dtypes.bfloat16
    x = np.asarray(full["L0_attn_x"][b], f32)
    x0 = np.zeros((EXT, D), f32)
    lo, hi = T0 - 256, T0 + 2176
    slo, shi = max(lo, 0), min(hi, 8192)
    x0[slo - lo:shi - lo] = x[slo:shi]
    kvmask = np.zeros((EXT,), f32)
    kvmask[slo - lo:shi - lo] = 1.0
    kvm_small = np.concatenate([kvmask[0:512], kvmask[2048:2432]])
    kk, qq = np.meshgrid(np.arange(128), np.arange(128), indexing="ij")
    m0 = (kk < qq).astype(f32) * -90.0
    m2 = (kk >= qq).astype(f32) * -90.0
    bmask = np.zeros((128, 2, 512), f32)
    for i in range(4):
        bmask[:, 0, 128 * i:128 * (i + 1)] = m0
        bmask[:, 1, 128 * i:128 * (i + 1)] = m2
    ramp = np.tile(np.arange(1, TC + 1, dtype=f32)[None, :], (128, 1))

    cp = np.zeros((128, 96), f32)
    cp[:, CP_BQ:CP_BQ + 8] = _fm_vec(full["L0_attn_bq"])
    bk = np.asarray(full["L0_attn_bk"], f32)
    bkd = np.zeros((128, 2), f32)
    for g in range(2):
        bkd[0:64, g] = bk[64 * g:64 * g + 64]
        bkd[64:128, g] = bk[64 * g:64 * g + 64]
    cp[:, CP_BKD:CP_BKD + 2] = bkd
    cp[:, CP_BV:CP_BV + 1] = np.asarray(full["L0_attn_bv"], f32)[:, None]
    cp[:, CP_BO:CP_BO + 8] = _fm_vec(full["L0_attn_bo"])
    cp[:, CP_BF:CP_BF + 8] = _fm_vec(full["L0_ffn_bias"])
    cp[:, CP_W:CP_W + 8] = _fm_vec(full["L1_attn_time_decay"])
    tmk = _fm_vec(np.asarray(full["L1_attn_time_mix_k"]).reshape(-1))
    tmv = _fm_vec(np.asarray(full["L1_attn_time_mix_v"]).reshape(-1))
    tmr = _fm_vec(np.asarray(full["L1_attn_time_mix_r"]).reshape(-1))
    cp[:, CP_TMK:CP_TMK + 8] = tmk
    cp[:, CP_TMV:CP_TMV + 8] = tmv
    cp[:, CP_TMR:CP_TMR + 8] = tmr
    cp[:, CP_MK1:CP_MK1 + 8] = 1.0 - tmk
    cp[:, CP_MV1:CP_MV1 + 8] = 1.0 - tmv
    cp[:, CP_MR1:CP_MR1 + 8] = 1.0 - tmr
    cp[:, CP_FM] = 0.0 if s == 0 else 1.0
    cp[:, CP_EPS] = EPS
    cp[:, CP_HALF] = 0.5
    cp2 = np.zeros((128, 64), f32)
    cp2[:, C2_LNG:C2_LNG + 8] = _fm_vec(full["L1_attn_ln_g"])
    cp2[:, C2_LNB:C2_LNB + 8] = _fm_vec(full["L1_attn_ln_b"])
    mur = _fm_vec(full["L1_ffn_mu_r"])
    muk = _fm_vec(full["L1_ffn_mu_k"])
    cp2[:, C2_MUR:C2_MUR + 8] = mur
    cp2[:, C2_MUK:C2_MUK + 8] = muk
    cp2[:, C2_MUR1:C2_MUR1 + 8] = 1.0 - mur
    cp2[:, C2_MUK1:C2_MUK1 + 8] = 1.0 - muk
    kap = np.zeros((8,), f32)
    kin = np.zeros((8,), f32)
    for r in range(8):
        if r // 4 == b and (r % 4) < s:
            kin[r] = 1.0
            kap[r] = 2048.0 * (s - 1 - (r % 4))
    cp2[:, C2_KAP:C2_KAP + 8] = np.tile(kap[None, :], (128, 1))
    cp2[:, C2_KIN:C2_KIN + 8] = np.tile(kin[None, :], (128, 1))

    wk = np.asarray(full["L0_attn_Wk"], f32)
    wkd = np.concatenate([wk[:, 0:64], wk[:, 0:64], wk[:, 64:128],
                          wk[:, 64:128]], axis=1)
    wkv = np.concatenate([wkd, np.asarray(full["L0_attn_Wv"], f32)], axis=1)
    perm = []
    for cc in range(8):
        g, i = cc // 4, cc % 4
        a, bb = 8 * g + 2 * i, 8 * g + 2 * i + 1
        perm.extend(range(64 * a, 64 * a + 64))
        perm.extend(range(64 * bb, 64 * bb + 64))
    wo = np.ascontiguousarray(np.asarray(full["L0_attn_Wo"], f32)[perm, :])

    tT = lambda w: np.ascontiguousarray(np.asarray(w, f32).T.astype(bf16))
    wbf = lambda w: np.asarray(w, f32).astype(bf16)
    return {
        "x0": x0,
        "wq": wbf(full["L0_attn_Wq"]), "wkv": wkv.astype(bf16),
        "wo": wo.astype(bf16),
        "wf": wbf(full["L0_ffn_weight"]),
        "wa1": wbf(full["L0_ffn_aux_weight1"]),
        "wa2": wbf(full["L0_ffn_aux_weight2"]),
        "l1wk": tT(full["L1_attn_Wk"]), "l1wv": tT(full["L1_attn_Wv"]),
        "l1wr": tT(full["L1_attn_Wr"]), "l1wo": tT(full["L1_attn_Wo"]),
        "fwr": tT(full["L1_ffn_W_r"]), "fwk": tT(full["L1_ffn_W_k"]),
        "fwv": tT(full["L1_ffn_W_v"]),
        "cpack": cp, "cpack2": cp2,
        "kvmask": np.tile(kvm_small[None, :], (128, 1)).astype(bf16),
        "identbf": np.eye(128, dtype=f32).astype(bf16),
        "bmask": bmask.astype(bf16),
        "rampbf": ramp.astype(bf16),
    }


def kernel(**inputs):
    nc, _ = build()
    in_maps = [_prep_core_inputs(c, inputs) for c in range(NCORES)]
    res = run_bass_kernel_spmd(nc, in_maps, core_ids=list(range(NCORES)))
    outs = []
    for b in range(2):
        outs.append(np.concatenate(
            [res.results[4 * b + s]["out"] for s in range(4)], axis=0))
    return np.stack(outs, axis=0)


# version 14
# speedup vs baseline: 1.0353x; 1.0353x over previous
import numpy as np
import ml_dtypes
from contextlib import ExitStack
import concourse.bass as bass
import concourse.mybir as mybir
from concourse.bass_utils import run_bass_kernel_spmd

F32 = mybir.dt.float32
BF16 = mybir.dt.bfloat16
AF = mybir.ActivationFunctionType
ALU = mybir.AluOpType
AXX = mybir.AxisListType.X

D = 1024
NCORES = 8
TC = 2048
NT_X = 19
EXT = NT_X * 128
FXW = 2304
FD, SW, FA = 0, 896, 1600
EPS = 1e-5
WEPS = 1e-8

ST_SUM, ST_SQ, ST_MU0, ST_VAR, ST_A, ST_B, ST_S1, ST_S2, ST_S3 = (
    0, 19, 38, 57, 76, 95, 114, 133, 152)
PO = 171
STATS_W = 344
CP_BQ, CP_BKD, CP_BV, CP_BO, CP_BF, CP_W = 0, 8, 10, 11, 19, 27
CP_TMK, CP_TMV, CP_TMR, CP_MK1, CP_MV1, CP_MR1, CP_FM = 35, 43, 51, 59, 67, 75, 83
CP_EPS, CP_HALF = 84, 85
C2_LNG, C2_LNB, C2_MUR, C2_MUK, C2_MUR1, C2_MUK1, C2_KAP, C2_KIN = (
    0, 8, 16, 24, 32, 40, 48, 56)
CA_NIN, CA_DIN, CA_NF, CA_DF, CA_T0 = 0, 8, 16, 24, 32

WSCHED = [("wkv", 0), ("wq", 1), ("wo", 0), ("wq", 1), ("wq", 1),
          ("wf", 1), ("wa1", 2), ("wa2", 0), ("wf", 1), ("wa1", 2),
          ("wa2", 0), ("wf", 1), ("wa1", 2), ("wa2", 0),
          ("l1wk", 1), ("l1wv", 0), ("l1wr", 1), ("l1wo", 0),
          ("fwr", 1), ("fwk", 0), ("fwv", 1), ("fwr", 0), ("fwk", 1),
          ("fwv", 0), ("fwr", 1), ("fwk", 0), ("fwv", 1), ("fwr", 0),
          ("fwk", 1), ("fwv", 0)]
WCOLS = {"wkv": 384}
WNAMES = ["wkv", "wq", "wo", "wf", "wa1", "wa2",
          "l1wk", "l1wv", "l1wr", "l1wo", "fwr", "fwk", "fwv"]

Y_PASSES = [[1, 2, 3, 4, 5], [6, 7, 8, 9, 10], [11, 12, 13, 14, 15], [16, 17]]
MIX_BOUNDS = [(256, 768), (768, 1408), (1408, 2048), (2048, 2304)]


def build(stage=4, dbg=()):
    dbgnames = [n for n, _ in dbg]
    nc = bass.Bass("TRN2", target_bir_lowering=False, debug=False)
    stack = ExitStack()
    T, V, S, G, SY = nc.tensor, nc.vector, nc.scalar, nc.gpsimd, nc.sync

    NTP = 4
    sems, cnt = {}, {}
    for s_ in (["ldx", "ldc", "ldw", "mm", "ev", "vv", "pp", "cc", "sp", "st",
                "dbg"] + [f"tp{i}" for i in range(NTP)]):
        sems[s_] = stack.enter_context(nc.semaphore(s_))
        cnt[s_] = 0

    def inc(instr, name, n=1):
        instr.then_inc(sems[name], n)
        cnt[name] += n
        return cnt[name]

    def wait(eng, name, val=None):
        eng.wait_ge(sems[name], cnt[name] if val is None else val)

    def gdma(semname, lag=0, **kw):
        G.wait_ge(sems[semname], max(0, cnt[semname] - 16 * lag))
        return inc(G.dma_start(**kw), semname, 16)

    def sb(name, shape, dtype):
        return stack.enter_context(nc.sbuf_tensor(name, shape, dtype))

    def inp(name, shape, dtype=F32):
        return nc.dram_tensor(name, shape, dtype, kind="ExternalInput").ap()

    x0_h = inp("x0", [EXT, D])
    out_h = nc.dram_tensor("out", [TC, D], F32, kind="ExternalOutput").ap()
    whost = {n: inp(n, [D, WCOLS.get(n, 1024)], BF16) for n in WNAMES}
    cpack_h = inp("cpack", [128, 96])
    cpack2_h = inp("cpack2", [128, 64])
    kvmask_h = inp("kvmask", [128, 896], BF16)
    bmask_h = inp("bmask", [128, 2, 512], BF16)
    identbf_h = inp("identbf", [128, 128], BF16)
    rampbf_h = inp("rampbf", [128, TC], BF16)
    cc_in = nc.dram_tensor("cc_in", [128, 16], F32).ap()
    cc_out = nc.dram_tensor("cc_out", [NCORES * 128, 16], F32).ap()
    spill_h = nc.dram_tensor("spill", [8, 2, 128, TC], F32).ap()
    dbg_h = {n: nc.dram_tensor("dbg_" + n, s, F32, kind="ExternalOutput").ap()
             for n, s in dbg}

    stream = sb("stream", [128, 18, 1024], F32)
    fmX = sb("fmX", [128, 8, FXW], BF16)
    fmB = sb("fmB", [128, 8, 800], BF16)
    fmC = sb("fmC", [128, 8, 800], BF16)
    ptm = sb("ptm", [128, 2, 1024], BF16)
    trash = sb("trash", [128, 1024], BF16)
    wb = [sb(f"wb{i}", [128, 8, 1024], BF16) for i in range(2)]
    tmS = sb("tmS", [128, 2, 1024], BF16)
    tmF = sb("tmF", [128, 1024], F32)
    stats = sb("stats", [128, STATS_W], F32)
    cpk = sb("cpk", [128, 96], F32)
    cpk2 = sb("cpk2", [128, 64], F32)
    etd = sb("etd", [128, 8], F32)
    gath = sb("gath", [128, 8, 16], F32)
    carry = sb("carry", [128, 48], F32)
    ylagc = sb("ylagc", [128, 16], BF16)

    psA = stack.enter_context(nc.psum_tensor("psA", [128, 1536], F32))
    psB = stack.enter_context(nc.psum_tensor("psB", [128, 1536], F32))
    psC = stack.enter_context(nc.psum_tensor("psC", [128, 512], F32))
    psD = stack.enter_context(nc.psum_tensor("psD", [128, 512], F32))

    arena = nc.auto_arena()
    k2 = arena.sbuf_tensor("k2", [128, 2, EXT], BF16)
    v_tm = arena.sbuf_tensor("v_tm", [128, NT_X, 128], BF16)
    probs = arena.sbuf_tensor("probs", [128, 1536], BF16)
    rden = arena.sbuf_tensor("rden", [128, 512], F32)
    kvm = arena.sbuf_tensor("kvm", [128, 896], BF16)
    bmk = arena.sbuf_tensor("bmk", [128, 2, 512], BF16)
    ones_bf = arena.sbuf_tensor("ones_bf", [128, 128], BF16)
    ident_bf = arena.sbuf_tensor("ident_bf", [128, 128], BF16)

    arenaF = nc.auto_arena()
    wbF = arenaF.sbuf_tensor("wbF", [128, 8, 1024], BF16)

    arena2 = nc.auto_arena()
    scr = arena2.sbuf_tensor("scr", [128, 4, 656], F32)
    rampb = arena2.sbuf_tensor("rampb", [128, TC], BF16)
    tr2 = arena2.sbuf_tensor("tr2", [128, 8, 656], BF16)

    nc.const_aps.register(F32, EPS, cpk[:, CP_EPS:CP_EPS + 1])
    nc.const_aps.register(F32, 0.5, cpk[:, CP_HALF:CP_HALF + 1])

    wload_next = [0]
    winst = {}
    wdep = []
    occ = [None, None, None]
    for i, (nm, b) in enumerate(WSCHED):
        wdep.append(0 if occ[b] is None else occ[b] + 1)
        occ[b] = i
    wfree = {}

    wbF_first = [True]

    def load_weights_upto(sched_idx):
        while wload_next[0] <= sched_idx:
            i = wload_next[0]
            nm, b = WSCHED[i]
            ncols = WCOLS.get(nm, 1024)
            if wdep[i] > 0:
                prev = wdep[i] - 1
                assert prev in wfree, f"load {i} dep {prev} not yet consumed"
                G.wait_ge(sems["mm"], wfree[prev])
            if b == 2 and wbF_first[0]:
                wait(G, "mm"); wait(G, "ev"); wait(G, "vv")
                wbF_first[0] = False
            gdma("ldw", out=wb[b][:, :, 0:ncols],
                 in_=whost[nm].rearrange("(c p) m -> p c m", p=128))
            winst[nm] = i
            wload_next[0] += 1

    def wuse(name):
        i = winst[name]
        return i, wb[WSCHED[i][1]]

    def wait_weight(eng, name):
        eng.wait_ge(sems["ldw"], 16 * (winst[name] + 1))

    def weight_done(name):
        wfree[winst[name]] = cnt["mm"]

    banks = [psA[:, 0:512], psA[:, 512:1024], psA[:, 1024:1536],
             psB[:, 0:512], psB[:, 512:1024], psB[:, 1024:1536],
             psD[:, 0:512]]
    bmark = [None] * 7
    bptr = [0]

    def take_bank():
        bi = bptr[0] % 7
        bptr[0] += 1
        if bmark[bi] is not None:
            T.wait_ge(sems[bmark[bi][0]], bmark[bi][1])
        return bi, banks[bi]

    def nslices(cols, width=512):
        out, o = [], 0
        while o < cols:
            w = min(width, cols - o)
            out.append((o, w))
            o += w
        return out

    tile_tp = {}
    tpc = [0]
    tp_rr = [0]
    slotmark = [None, None]
    trash_war = [None]

    def tp_wait(eng, marks):
        byse = {}
        for s_, v in marks:
            byse[s_] = max(byse.get(s_, 0), v)
        for s_, v in byse.items():
            eng.wait_ge(sems[s_], v)

    def tpsnap():
        return ("tpsnap", tuple(cnt[f"tp{i}"] for i in range(NTP)))

    def war_wait(eng, war):
        if war is None:
            return
        if war[0] == "tpsnap":
            for i, v in enumerate(war[1]):
                if v:
                    eng.wait_ge(sems[f"tp{i}"], v)
        else:
            eng.wait_ge(sems[war[0]], war[1])

    def tp_dma(out, in_):
        s_ = f"tp{tp_rr[0] % NTP}"
        tp_rr[0] += 1
        SY.wait_ge(sems[s_], cnt[s_])
        it = SY.dma_start(out=out, in_=in_, transpose=True)
        return (s_, inc(it, s_, 16))

    def sweep_tile(src_ap, dst3, a_col, b_col, s_wait=None, sy_wait=None):
        slot = tpc[0] % 2
        if slotmark[slot] is not None:
            tp_wait(S, [slotmark[slot]])
        if s_wait is not None:
            for sv in (s_wait if isinstance(s_wait, list) else [s_wait]):
                war_wait(S, sv)
        ia = S.activation(tmS[:, slot, :], src_ap, AF.Identity,
                          scale=a_col, bias=b_col)
        ec = inc(ia, "ev", 1)
        wait(SY, "ev", ec)
        if sy_wait is not None:
            war_wait(SY, sy_wait)
        mark = tp_dma(dst3, tmS[:, slot, :])
        slotmark[slot] = mark
        tpc[0] += 1
        return mark

    def tile_wait(eng, tiles):
        tp_wait(eng, [tile_tp[t] for t in tiles])

    def ln_rows(n, base, reg=0, with_mu=True, keep_mu0=False):
        o = reg
        sl = lambda c: stats[:, o + c + base:o + c + base + n]
        wait(V, "ev"); wait(V, "vv"); wait(V, "pp")
        if with_mu:
            mu = stats[:, ST_MU0 + base:ST_MU0 + base + n] if keep_mu0 \
                else sl(ST_S3)
            V.tensor_scalar(mu, sl(ST_SUM), 1.0 / 1024.0, None, ALU.mult)
            V.drain()
        else:
            mu = stats[:, ST_MU0 + base:ST_MU0 + base + n]
        V.tensor_tensor(sl(ST_S2), mu, mu, ALU.mult)
        V.drain()
        V.scalar_tensor_tensor(sl(ST_VAR), sl(ST_SQ), 1.0 / 1024.0,
                               sl(ST_S2), ALU.mult, ALU.subtract)
        V.drain()
        vq = inc(V.nop(), "vv", 1)
        wait(S, "vv", vq)
        S.activation(sl(ST_S1), sl(ST_VAR), AF.Sqrt, bias=EPS)
        S.drain()
        eq = inc(S.nop(), "ev", 1)
        wait(V, "ev", eq)
        V.reciprocal(sl(ST_A), sl(ST_S1))
        V.drain()
        V.scalar_tensor_tensor(sl(ST_B), mu, -1.0, sl(ST_A),
                               ALU.mult, ALU.mult)
        V.drain()
        return inc(V.nop(), "vv", 1)

    def proj(wname, rhs_h, rhs_off, cols, mlist, evac, last_use=False,
             slice_wait=None, pre_wait=None, drain=False):
        wait_weight(T, wname)
        _, wbf_ = wuse(wname)
        out_marks = {}
        first = True
        for (n0, w) in nslices(cols):
            if slice_wait is not None:
                for sv in slice_wait(n0, w):
                    war_wait(T, sv)
            if first and pre_wait:
                for sv in pre_wait:
                    war_wait(T, sv)
            first = False
            for m in mlist:
                bi, ps = take_bank()
                last = None
                for k in range(8):
                    last = T.matmul(ps[:, 0:w],
                                    wbf_[:, k, 128 * m:128 * m + 128],
                                    rhs_h[:, k, rhs_off + n0:rhs_off + n0 + w],
                                    start=(k == 0), stop=(k == 7))
                mk = inc(last, "mm", 1)
                em = evac(m, n0, w, ps, mk)
                bmark[bi] = em
                out_marks[n0] = em
                if drain:
                    drain_thunks(1)
        if last_use:
            weight_done(wname)
        return out_marks

    ptm_war = [None, []]
    stream_mark = {}
    defq = []

    def drain_thunks(n=1):
        for _ in range(min(n, len(defq))):
            defq.pop(0)()

    def postln_group(tiles, src_h, src_off, ew, fold=True):
        ntl = len(tiles)
        war_wait(SY, ew)
        if ptm_war[0] is not None:
            for wv in ptm_war[0]:
                war_wait(SY, wv)
        gm = []
        for c in range(8):
            gm.append(tp_dma(ptm[:, 0:ntl, 128 * c:128 * c + 128],
                             src_h[:, c, src_off:src_off + 128 * ntl]))
        tp_wait(S, gm)
        for i in range(ntl):
            S.activation(ptm[:, i, :], ptm[:, i, :], AF.Identity,
                         accum_out=stats[:, PO + ST_SUM + i:PO + ST_SUM + i + 1])
            S.drain()
        se = inc(S.nop(), "ev", 1)
        tp_wait(V, gm)
        wait(V, "ev", se)
        if trash_war[0] is not None:
            tp_wait(V, [trash_war[0]])
        for i in range(ntl):
            V.scalar_tensor_tensor(trash[:, :], ptm[:, i, :], 1.0,
                                   ptm[:, i, :], ALU.mult, ALU.mult,
                                   accum_out=stats[:, PO + ST_SQ + i:
                                                   PO + ST_SQ + i + 1])
            V.drain()
        inc(V.nop(), "vv", 1)
        lnv = ln_rows(ntl, 0, reg=PO, with_mu=True)
        wait(V, "ev", se)
        for i, t in enumerate(tiles):
            wait(V, "vv", lnv)
            V.scalar_tensor_tensor(stream[:, t, :], ptm[:, i, :],
                                   stats[:, PO + ST_A + i:PO + ST_A + i + 1],
                                   stream[:, t, :], ALU.mult, ALU.add)
            V.drain()
            V.tensor_scalar(stream[:, t, :], stream[:, t, :],
                            stats[:, PO + ST_B + i:PO + ST_B + i + 1],
                            None, ALU.add)
            V.drain()
            if fold:
                V.scalar_tensor_tensor(trash[:, :], stream[:, t, :], 1.0,
                                       stream[:, t, :], ALU.mult, ALU.mult,
                                       accum_out=stats[:, ST_SQ + t:
                                                       ST_SQ + t + 1])
                V.drain()
            stream_mark[t] = ("vv", inc(V.nop(), "vv", 1))
        ptm_war[0] = [("ev", se), stream_mark[tiles[-1]]]

    def postln_pass(tiles, src_h, src_off_base, ew, fold=True):
        o = 0
        while o < len(tiles):
            grp = tiles[o:o + 2]
            postln_group(grp, src_h, src_off_base + 128 * o, ew, fold=fold)
            o += 2

    def postln_deferred(tiles_all, src_h, src_off_base, ew, fold=True,
                        eng="V", sweep_to=False, mini_ln=False, war_out=None):
        E = G if eng == "G" else V
        esem = "pp" if eng == "G" else "vv"
        gdst = trash[:, :] if eng == "V" else tmF[:, :]
        if war_out is not None:
            war_out[0] = []

        def mk_tp(o, grp):
            def f():
                ntl = len(grp)
                war_wait(SY, ew)
                if ptm_war[0] is not None:
                    for wv in ptm_war[0]:
                        war_wait(SY, wv)
                gm = []
                for c in range(8):
                    gm.append(tp_dma(
                        ptm[:, 0:ntl, 128 * c:128 * c + 128],
                        src_h[:, c, src_off_base + 128 * o:
                              src_off_base + 128 * o + 128 * ntl]))
                ptm_war[1] = gm
                if war_out is not None:
                    war_out[0].extend(gm)
            return f

        def mk_stats(i):
            def f():
                tp_wait(E, ptm_war[1])
                if eng == "V" and trash_war[0] is not None:
                    tp_wait(E, [trash_war[0]])
                E.tensor_scalar(gdst, ptm[:, i, :], 1.0, 0.0, ALU.mult,
                                ALU.add,
                                accum_out=stats[:, PO + ST_SUM + i:
                                                PO + ST_SUM + i + 1])
                E.drain()
                E.scalar_tensor_tensor(gdst, ptm[:, i, :], 1.0,
                                       ptm[:, i, :], ALU.mult, ALU.mult,
                                       accum_out=stats[:, PO + ST_SQ + i:
                                                       PO + ST_SQ + i + 1])
                E.drain()
                inc(E.nop(), esem, 1)
            return f

        def mk_ln(ntl):
            def f():
                lnv = ln_rows(ntl, 0, reg=PO, with_mu=True)
                ptm_war.append(lnv)
            return f

        def mk_apply(i, t, last):
            def f():
                lnv = ptm_war[-1]
                E.wait_ge(sems["vv"], lnv)
                E.scalar_tensor_tensor(stream[:, t, :], ptm[:, i, :],
                                       stats[:, PO + ST_A + i:
                                             PO + ST_A + i + 1],
                                       stream[:, t, :], ALU.mult, ALU.add)
                E.drain()
                E.tensor_scalar(stream[:, t, :], stream[:, t, :],
                                stats[:, PO + ST_B + i:PO + ST_B + i + 1],
                                0.0, ALU.add, ALU.add)
                E.drain()
                if fold:
                    E.scalar_tensor_tensor(gdst, stream[:, t, :], 1.0,
                                           stream[:, t, :], ALU.mult,
                                           ALU.mult,
                                           accum_out=stats[:, ST_SQ + t:
                                                           ST_SQ + t + 1])
                    E.drain()
                stream_mark[t] = (esem, inc(E.nop(), esem, 1))
                if last:
                    ptm_war[0] = [stream_mark[t]]
            return f

        for o in range(0, len(tiles_all), 2):
            grp = tiles_all[o:o + 2]
            defq.append(mk_tp(o, grp))
            for i in range(len(grp)):
                defq.append(mk_stats(i))
            defq.append(mk_ln(len(grp)))
            for i, t in enumerate(grp):
                defq.append(mk_apply(i, t, i == len(grp) - 1))

        if mini_ln:
            def f_mini2():
                stats_rdy[0] = ln_rows(len(tiles_all), tiles_all[0],
                                       with_mu=False)
            defq.append(f_mini2)

        if sweep_to:
            def f_mini():
                lnm = ln_rows(len(tiles_all), tiles_all[0], with_mu=False)
                ptm_war.append(("mini", lnm))
            defq.append(f_mini)

            def mk_sweep(t):
                def f():
                    lnm = ptm_war[-1][1]
                    tile_tp[t] = sweep_tile(
                        stream[:, t, :], fmX[:, :, 128 * t:128 * t + 128],
                        stats[:, ST_A + t:ST_A + t + 1],
                        stats[:, ST_B + t:ST_B + t + 1],
                        s_wait=[("vv", lnm), stream_mark[t]])
                return f
            for t in tiles_all:
                defq.append(mk_sweep(t))

    def dump(name, src):
        if name in dbgnames:
            drain_thunks(len(defq))
            wait(G, "vv"); wait(G, "ev"); wait(G, "mm"); wait(G, "pp")
            gdma("dbg", out=dbg_h[name], in_=src)
            wait(T, "dbg"); wait(S, "dbg"); wait(V, "dbg"); wait(SY, "dbg")

    wb.append(wbF)

    with nc.Block() as block:
        arena.allocate()

        gdma("ldx", out=stream[:, 0:9, :],
             in_=x0_h[0:1152, :].rearrange("(t p) d -> p t d", p=128))
        gdma("ldx", out=stream[:, 9:18, :],
             in_=x0_h[1152:2304, :].rearrange("(t p) d -> p t d", p=128))
        gdma("ldx", out=tmF[:, :], in_=x0_h[2304:2432, :])
        for dst, src in ((cpk[:, :], cpack_h), (cpk2[:, :], cpack2_h),
                         (kvm[:, :], kvmask_h), (bmk[:, :, :], bmask_h),
                         (ident_bf[:, :], identbf_h)):
            gdma("ldc", out=dst, in_=src)
        load_weights_upto(1)

        V.memset(ones_bf[:, :], 1.0)
        V.memset(carry[:, :], 0.0)
        V.drain()
        inc(V.nop(), "vv", 1)
        wait(S, "ldc")
        S.activation(etd[:, 0:8], cpk[:, CP_W:CP_W + 8], AF.Exp)
        S.drain()
        inc(S.nop(), "ev", 1)

        for t in range(19):
            lv = 16 if t < 9 else (32 if t < 18 else 48)
            src = stream[:, t, :] if t < 18 else tmF[:, :]
            wait(S, "ldx", lv)
            S.activation(tmS[:, 0, :], src, AF.Identity,
                         accum_out=stats[:, ST_SUM + t:ST_SUM + t + 1])
            S.drain()
            wait(V, "ldx", lv)
            V.scalar_tensor_tensor(trash[:, :], src, 1.0, src,
                                   ALU.mult, ALU.mult,
                                   accum_out=stats[:, ST_SQ + t:ST_SQ + t + 1])
            V.drain()
        inc(S.nop(), "ev", 1)
        inc(V.nop(), "vv", 1)
        ln1 = ln_rows(19, 0, with_mu=True, keep_mu0=True)

        for t in range(19):
            src = stream[:, t, :] if t < 18 else tmF[:, :]
            if t < 18:
                dst3 = fmX[:, :, 128 * t:128 * t + 128]
            else:
                dst3 = ptm[:, 0, :].rearrange("p (c q) -> p c q", q=128)
            tile_tp[t] = sweep_tile(src, dst3,
                                    stats[:, ST_A + t:ST_A + t + 1],
                                    stats[:, ST_B + t:ST_B + t + 1],
                                    s_wait=("vv", ln1))

        wait_weight(T, "wkv")
        _, wkvb = wuse("wkv")
        wait(V, "ldc")
        vtp_last = [None]
        k2_done = [0]
        kv_slices = [(0, 512), (512, 512), (1024, 512), (1536, 512),
                     (2048, 256), (2304, 128)]
        for si, (n0, w) in enumerate(kv_slices):
            if n0 < 2304:
                tile_wait(T, range(n0 // 128, (n0 + w - 1) // 128 + 1))
            else:
                tp_wait(T, [tile_tp[18]])
            edge = (si == 0 or si >= 4)
            if si == 0:
                kvs = kvm[:, 0:512]
            elif si == 4:
                kvs = kvm[:, 512:768]
            elif si == 5:
                kvs = kvm[:, 768:896]
            else:
                kvs = None
            for m in range(3):
                bi, ps = take_bank()
                last = None
                for k in range(8):
                    rr = (fmX[:, k, n0:n0 + w] if n0 < 2304
                          else ptm[:, 0, 128 * k:128 * k + 128])
                    last = T.matmul(ps[:, 0:w], wkvb[:, k, 128 * m:128 * m + 128],
                                    rr, start=(k == 0), stop=(k == 7))
                mk = inc(last, "mm", 1)
                wait(V, "mm", mk)
                if m < 2:
                    if edge:
                        V.scalar_tensor_tensor(k2[:, m, n0:n0 + w], ps[:, 0:w],
                                               cpk[:, CP_BKD + m:CP_BKD + m + 1],
                                               kvs, ALU.add, ALU.mult)
                    else:
                        V.tensor_scalar(k2[:, m, n0:n0 + w], ps[:, 0:w],
                                        cpk[:, CP_BKD + m:CP_BKD + m + 1],
                                        None, ALU.add)
                    V.drain()
                    em = ("vv", inc(V.nop(), "vv", 1))
                    k2_done[0] = em[1]
                else:
                    if vtp_last[0] is not None:
                        tp_wait(V, [vtp_last[0]])
                    if edge:
                        V.scalar_tensor_tensor(trash[:, 0:w], ps[:, 0:w],
                                               cpk[:, CP_BV:CP_BV + 1],
                                               kvs, ALU.add, ALU.mult)
                    else:
                        V.tensor_scalar(trash[:, 0:w], ps[:, 0:w],
                                        cpk[:, CP_BV:CP_BV + 1], None, ALU.add)
                    V.drain()
                    vv_ = inc(V.nop(), "vv", 1)
                    wait(SY, "vv", vv_)
                    vtp_last[0] = tp_dma(v_tm[:, n0 // 128:(n0 + w) // 128, :],
                                         trash[:, 0:w])
                    em = ("vv", vv_)
                bmark[bi] = em
        weight_done("wkv")
        trash_war[0] = vtp_last[0]
        load_weights_upto(2)
        dump("k2", k2[:, :, :])
        dump("stats0", stats[:, 0:133])
        if "vtm" in dbgnames:
            tp_wait(G, [vtp_last[0]])
            gdma("dbg", out=dbg_h["vtm"], in_=v_tm[:, :, :])
            wait(T, "dbg"); wait(S, "dbg"); wait(V, "dbg"); wait(SY, "dbg")

        q_passes = [list(range(1, 7)), list(range(7, 13)), list(range(13, 18))]
        wq_idx = [1, 3, 4]
        att = {"exp_free": [0, 0], "evac_free": 0, "recip_free": 0}
        for pi, tiles in enumerate(q_passes):
            ncols = 128 * len(tiles)
            poff = 128 * tiles[0]
            load_weights_upto(wq_idx[pi])
            lastp = (pi == len(q_passes) - 1)

            def evac_q(m, n0, w, ps, mk):
                wait(S, "mm", mk)
                S.activation(fmB[:, m, n0:n0 + w], ps[:, 0:w], AF.Identity,
                             bias=cpk[:, CP_BQ + m:CP_BQ + m + 1])
                S.drain()
                return ("ev", inc(S.nop(), "ev", 1))

            qm = proj("wq", fmX, poff, ncols, list(range(8)), evac_q,
                      last_use=True,
                      slice_wait=lambda n0, w: [
                          tile_tp[t] for t in
                          range((poff + n0) // 128,
                                (poff + n0 + w - 1) // 128 + 1)])
            if lastp and stage >= 2:
                load_weights_upto(5)
            if pi == 0:
                dump("q0", fmB[:, :, 0:768])

            wq_last = qm[max(qm)]
            T.wait_ge(sems["ev"], wq_last[1])
            T.wait_ge(sems["vv"], k2_done[0])
            wait(T, "ldc")
            tp_wait(T, [vtp_last[0]])
            for bi in range(7):
                if bmark[bi] is not None:
                    T.wait_ge(sems[bmark[bi][0]], bmark[bi][1])
                    bmark[bi] = None
            kc_map = [(0, -1, 0), (1, 1, 1), (2, 0, None)]

            def att_scores(j, g, qoff, par):
                pssc = psA if par == 0 else psB
                wait(T, "ev", att["exp_free"][par])
                sc_last = None
                for (r, dj, msk) in kc_map:
                    kt = j + dj
                    sc_last = T.matmul(
                        pssc[:, 512 * r:512 * r + 512],
                        k2[64 * par:64 * par + 64, g,
                           128 * kt:128 * kt + 128],
                        fmB[64 * par:64 * par + 64, 4 * g:4 * g + 4,
                            qoff:qoff + 128],
                        start=True, stop=(msk is None))
                    if msk is not None:
                        sc_last = T.matmul(
                            pssc[:, 512 * r:512 * r + 512],
                            ident_bf[:, :], bmk[:, msk, :],
                            start=False, stop=True)
                return inc(sc_last, "mm", 1)

            def att_exp(par, mk, probs_war):
                pssc = psA if par == 0 else psB
                wait(S, "mm", max(mk, probs_war))
                S.activation(probs[:, :], pssc[:, :], AF.Exp)
                S.drain()
                emk = inc(S.nop(), "ev", 1)
                att["exp_free"][par] = emk
                return emk

            def att_denav(j, g, par, emk):
                wait(T, "ev", emk)
                if par == 0:
                    wait(T, "vv", att["recip_free"])
                    wait(T, "vv", att["evac_free"])
                dlast = None
                for (r, dj, msk) in kc_map:
                    dlast = T.matmul(
                        psC[64 * par:64 * par + 64, 0:512],
                        ones_bf[:, 0:64],
                        probs[:, 512 * r:512 * r + 512],
                        start=(r == 0), stop=(r == 2),
                        tile_position=(0, 64 * par))
                inc(dlast, "mm", 1)
                avlast = None
                for (r, dj, msk) in kc_map:
                    kt = j + dj
                    avlast = T.matmul(
                        psD[64 * par:64 * par + 64, 0:512],
                        v_tm[:, kt, 64 * g:64 * g + 64],
                        probs[:, 512 * r:512 * r + 512],
                        start=(r == 0), stop=(r == 2),
                        tile_position=(0, 64 * par))
                return inc(avlast, "mm", 1)

            probs_war = [0]
            for ji, j in enumerate(tiles):
                qoff = 128 * ji
                for g in range(2):
                    mk0 = att_scores(j, g, qoff, 0)
                    emk0 = att_exp(0, mk0, probs_war[0])
                    mk1 = att_scores(j, g, qoff, 1)
                    amk0 = att_denav(j, g, 0, emk0)
                    emk1 = att_exp(1, mk1, amk0)
                    amk1 = att_denav(j, g, 1, emk1)
                    probs_war[0] = amk1
                    wait(V, "mm", amk1)
                    V.reciprocal(rden[:, :], psC[:, 0:512])
                    V.drain()
                    att["recip_free"] = inc(V.nop(), "vv", 1)
                    V.tensor_tensor(
                        fmC[:, 4 * g:4 * g + 4, qoff:qoff + 128],
                        psD[:, 0:512].rearrange("p (i q) -> p i q", q=128),
                        rden[:, :].rearrange("p (i q) -> p i q", q=128),
                        ALU.mult)
                    V.drain()
                    att["evac_free"] = inc(V.nop(), "vv", 1)
                    drain_thunks(1)
            if pi == 0:
                dump("at0", fmC[:, :, 0:768])
            for bi in range(3):
                bmark[bi] = ("ev", att["exp_free"][0])
            for bi in range(3, 6):
                bmark[bi] = ("ev", att["exp_free"][1])
            bmark[6] = ("vv", att["evac_free"])

            def evac_p(m, n0, w, ps, mk, poff=poff):
                wait(S, "mm", mk)
                S.activation(fmX[:, m, poff + n0:poff + n0 + w], ps[:, 0:w],
                             AF.Identity, bias=cpk[:, CP_BO + m:CP_BO + m + 1])
                S.drain()
                return ("ev", inc(S.nop(), "ev", 1))

            wm = proj("wo", fmC, 0, ncols, list(range(8)), evac_p,
                      last_use=lastp, drain=True,
                      pre_wait=[("vv", att["evac_free"])])
            wo_last = wm[max(wm)]
            postln_deferred(tiles, fmX, poff, wo_last, fold=True, eng="V",
                            sweep_to=(stage >= 2))

        arena.free()
        arenaF.allocate()
        if stage >= 2:
            load_weights_upto(6)
        dump("x1", stream[:, :, :])

        if stage >= 2:
            f_passes = [list(range(1, 7)), list(range(7, 13)),
                        list(range(13, 18))]
            for pi, tiles in enumerate(f_passes):
                ncols = 128 * len(tiles)
                poff = 128 * tiles[0]
                load_weights_upto(5 + 3 * pi)

                def evac_h(dst):
                    def e(m, n0, w, ps, mk):
                        wait(S, "mm", mk)
                        S.activation(dst[:, m, n0:n0 + w], ps[:, 0:w],
                                     AF.Identity)
                        S.drain()
                        return ("ev", inc(S.nop(), "ev", 1))
                    return e

                m1 = proj("wf", fmX, poff, ncols, list(range(8)),
                          evac_h(fmB), last_use=True, drain=True,
                          slice_wait=lambda n0, w: [
                              tile_tp[t] for t in
                              range((poff + n0) // 128,
                                    (poff + n0 + w - 1) // 128 + 1)])
                load_weights_upto(6 + 3 * pi)
                m2 = proj("wa1", fmB, 0, ncols, list(range(8)),
                          evac_h(fmC), last_use=True, drain=True,
                          slice_wait=lambda n0, w: [m1[n0]])
                load_weights_upto(7 + 3 * pi)

                def evac_tanh(m, n0, w, ps, mk, poff=poff):
                    wait(S, "mm", mk)
                    S.activation(fmX[:, m, poff + n0:poff + n0 + w], ps[:, 0:w],
                                 AF.Tanh, bias=cpk[:, CP_BF + m:CP_BF + m + 1])
                    S.drain()
                    return ("ev", inc(S.nop(), "ev", 1))

                m3 = proj("wa2", fmC, 0, ncols, list(range(8)), evac_tanh,
                          last_use=True, drain=True,
                          slice_wait=lambda n0, w: [m2[n0]])
                postln_deferred(tiles, fmX, poff, m3[max(m3)], fold=True,
                                eng="V")
            drain_thunks(len(defq))

            dump("x2", stream[:, :, :])

        fd_war = [None]
        fa_war = [None]
        fc_war = [None]
        fb_war = [[]]
        tr2_war = [0]

        def emit_y_pass(tiles, gcol=None, bcol=None, mask255=False,
                        restore_lag=False):
            Wp = 128 * len(tiles)
            for i, t in enumerate(tiles):
                tile_tp[t] = sweep_tile(
                    stream[:, t, :],
                    fmX[:, :, FD + 128 * (i + 1):FD + 128 * (i + 2)],
                    stats[:, ST_A + t:ST_A + t + 1],
                    stats[:, ST_B + t:ST_B + t + 1],
                    s_wait=[("vv", stats_rdy[0]),
                            stream_mark.get(t, ("vv", 0))],
                    sy_wait=(fd_war[0] if i == 0 else None))
            tp_wait(V, [tile_tp[t] for t in tiles])
            if gcol is not None:
                for c in range(8):
                    V.tensor_scalar(fmX[:, c, FD + 128:FD + 128 + Wp],
                                    fmX[:, c, FD + 128:FD + 128 + Wp],
                                    cpk2[:, gcol + c:gcol + c + 1],
                                    cpk2[:, bcol + c:bcol + c + 1],
                                    ALU.mult, ALU.add)
                V.drain()
            if mask255:
                i255 = FD + 255 - 128 * tiles[0] + 128
                V.tensor_scalar(fmX[:, :, i255:i255 + 1],
                                fmX[:, :, i255:i255 + 1],
                                cpk[:, CP_FM:CP_FM + 1], None, ALU.mult)
                V.drain()
            if restore_lag:
                V.tensor_copy(fmX[:, :, FD + 127:FD + 128],
                              ylagc[:, 0:8].rearrange("p (c one) -> p c one",
                                                      one=1))
                V.drain()
            V.tensor_copy(ylagc[:, 0:8].rearrange("p (c one) -> p c one",
                                                  one=1),
                          fmX[:, :, FD + 128 + Wp - 1:FD + 128 + Wp])
            V.drain()
            return inc(V.nop(), "vv", 1)

        def emit_mix(dst_off, tiles, b0, b1, ctensor, tm_c, tm1_c, war):
            yb = 128 * tiles[0] - 128
            i0, i1 = b0 - yb, b1 - yb
            wait(V, "vv")
            war_wait(V, war[0])
            for c in range(8):
                V.tensor_scalar(fmX[:, c, dst_off:dst_off + i1 - i0],
                                fmX[:, c, FD + i0:FD + i1],
                                ctensor[:, tm_c + c:tm_c + c + 1], None,
                                ALU.mult)
            V.drain()
            for c in range(8):
                V.scalar_tensor_tensor(fmX[:, c, dst_off:dst_off + i1 - i0],
                                       fmX[:, c, FD + i0 - 1:FD + i1 - 1],
                                       ctensor[:, tm1_c + c:tm1_c + c + 1],
                                       fmX[:, c, dst_off:dst_off + i1 - i0],
                                       ALU.mult, ALU.add)
            V.drain()
            return inc(V.nop(), "vv", 1)

        def emit_mix_fc(tiles, b0, b1, tm_c, tm1_c):
            yb = 128 * tiles[0] - 128
            i0, i1 = b0 - yb, b1 - yb
            wait(V, "vv")
            war_wait(V, fc_war[0])
            for c in range(8):
                V.tensor_scalar(fmC[:, c, 0:i1 - i0],
                                fmX[:, c, FD + i0:FD + i1],
                                cpk[:, tm_c + c:tm_c + c + 1], None, ALU.mult)
            V.drain()
            for c in range(8):
                V.scalar_tensor_tensor(fmC[:, c, 0:i1 - i0],
                                       fmX[:, c, FD + i0 - 1:FD + i1 - 1],
                                       cpk[:, tm1_c + c:tm1_c + c + 1],
                                       fmC[:, c, 0:i1 - i0],
                                       ALU.mult, ALU.add)
            V.drain()
            return inc(V.nop(), "vv", 1)

        stats_rdy = [0]

        if stage >= 3:
            wait(G, "vv"); wait(G, "ev"); wait(G, "mm")
            arenaF.free()
            arena2.allocate()
            gdma("ldc", out=rampb[:, :], in_=rampbf_h)
            load_weights_upto(15)
            fa_war[0] = tpsnap()
            fd_war[0] = tpsnap()
            fc_war[0] = ("mm", cnt["mm"])

            n, base = 17, 1
            sl = lambda c: stats[:, c + base:c + base + n]
            wait(V, "vv"); wait(V, "ev")
            mu0s = stats[:, ST_MU0 + base:ST_MU0 + base + n]
            V.tensor_tensor(sl(ST_S2), mu0s, mu0s, ALU.mult)
            V.drain()
            V.scalar_tensor_tensor(sl(ST_VAR), sl(ST_SQ), 1.0 / 1024.0,
                                   sl(ST_S2), ALU.mult, ALU.subtract)
            V.drain()
            vq = inc(V.nop(), "vv", 1)
            wait(S, "vv", vq)
            S.activation(sl(ST_S1), sl(ST_VAR), AF.Sqrt, bias=EPS)
            S.drain()
            eq = inc(S.nop(), "ev", 1)
            wait(V, "ev", eq)
            V.reciprocal(sl(ST_A), sl(ST_S1))
            V.drain()
            V.tensor_tensor(sl(ST_S2), sl(ST_A), sl(ST_A), ALU.mult)
            V.drain()
            V.tensor_tensor(sl(ST_S2), sl(ST_S2), sl(ST_VAR), ALU.mult)
            V.drain()
            vq = inc(V.nop(), "vv", 1)
            wait(S, "vv", vq)
            S.activation(sl(ST_S1), sl(ST_S2), AF.Sqrt, bias=EPS)
            S.drain()
            eq = inc(S.nop(), "ev", 1)
            wait(V, "ev", eq)
            V.reciprocal(sl(ST_S3), sl(ST_S1))
            V.drain()
            V.tensor_tensor(sl(ST_A), sl(ST_A), sl(ST_S3), ALU.mult)
            V.drain()
            V.scalar_tensor_tensor(sl(ST_B), mu0s, -1.0, sl(ST_A),
                                   ALU.mult, ALU.mult)
            V.drain()
            stats_rdy[0] = inc(V.nop(), "vv", 1)

            spill_mark = [0, 0]
            for pi, tiles in enumerate(Y_PASSES):
                b0, b1 = MIX_BOUNDS[pi]
                ncols = b1 - b0
                lastp = (pi == len(Y_PASSES) - 1)
                ym = emit_y_pass(tiles, gcol=C2_LNG, bcol=C2_LNB,
                                 mask255=(pi == 0), restore_lag=(pi > 0))
                km = emit_mix(FA, tiles, b0, b1, cpk, CP_TMK, CP_MK1, fa_war)
                vm = emit_mix_fc(tiles, b0, b1, CP_TMV, CP_MV1)
                fd_war[0] = ("vv", vm)
                wait(T, "vv", vm)
                wait_weight(T, "l1wk"); wait_weight(T, "l1wv")
                _, wkb = wuse("l1wk")
                _, wvb = wuse("l1wv")
                kv_mm = 0
                for c in range(8):
                    sa, sb_ = (0, 1) if c % 2 == 0 else (2, 3)
                    klast = None
                    for (n0, w) in nslices(ncols):
                        bi, ps = take_bank()
                        for k in range(8):
                            klast = T.matmul(ps[:, 0:w],
                                             wkb[:, k, 128 * c:128 * c + 128],
                                             fmX[:, k, FA + n0:FA + n0 + w],
                                             start=(k == 0), stop=(k == 7))
                        mk = inc(klast, "mm", 1)
                        wait(S, "mm", mk)
                        S.wait_ge(sems["sp"], spill_mark[c % 2])
                        S.activation(scr[:, sa, n0:n0 + w], ps[:, 0:w], AF.Exp)
                        S.drain()
                        bmark[bi] = ("ev", inc(S.nop(), "ev", 1))
                    vlast = None
                    for (n0, w) in nslices(ncols):
                        bi, ps = take_bank()
                        for k in range(8):
                            vlast = T.matmul(ps[:, 0:w],
                                             wvb[:, k, 128 * c:128 * c + 128],
                                             fmC[:, k, n0:n0 + w],
                                             start=(k == 0), stop=(k == 7))
                        mk = inc(vlast, "mm", 1)
                        wait(S, "mm", mk)
                        S.activation(scr[:, sb_, n0:n0 + w], ps[:, 0:w],
                                     AF.Identity)
                        S.drain()
                        bmark[bi] = ("ev", inc(S.nop(), "ev", 1))
                    kv_mm = cnt["mm"]
                    if lastp and c == 7:
                        weight_done("l1wk")
                        weight_done("l1wv")
                    wait(V, "ev")
                    V.tensor_tensor(scr[:, sb_, 0:ncols], scr[:, sa, 0:ncols],
                                    scr[:, sb_, 0:ncols], ALU.mult)
                    V.drain()
                    V.tensor_tensor_scan(scr[:, sb_, 0:ncols],
                                         etd[:, c:c + 1].broadcast_to(
                                             (128, ncols)),
                                         scr[:, sb_, 0:ncols],
                                         carry[:, CA_NF + c:CA_NF + c + 1],
                                         ALU.mult, ALU.add)
                    V.drain()
                    V.tensor_tensor_scan(scr[:, sa, 0:ncols],
                                         etd[:, c:c + 1].broadcast_to(
                                             (128, ncols)),
                                         scr[:, sa, 0:ncols],
                                         carry[:, CA_DF + c:CA_DF + c + 1],
                                         ALU.mult, ALU.add)
                    V.drain()
                    V.tensor_copy(carry[:, CA_NF + c:CA_NF + c + 1],
                                  scr[:, sb_, ncols - 1:ncols])
                    V.tensor_copy(carry[:, CA_DF + c:CA_DF + c + 1],
                                  scr[:, sa, ncols - 1:ncols])
                    V.drain()
                    svk = inc(V.nop(), "vv", 1)
                    G.wait_ge(sems["vv"], svk)
                    o0 = b0 - 256
                    gdma("sp", out=spill_h[c, 0, :, o0:o0 + ncols],
                         in_=scr[:, sb_, 0:ncols])
                    spill_mark[c % 2] = gdma(
                        "sp", out=spill_h[c, 1, :, o0:o0 + ncols],
                        in_=scr[:, sa, 0:ncols])
                fa_war[0] = ("mm", kv_mm)
                fc_war[0] = ("mm", kv_mm)

            wait(G, "vv")
            gdma("sp", out=cc_in, in_=carry[:, CA_NF:CA_NF + 16])
            wait(G, "sp")
            ic = G.collective_compute(
                "AllGather", ALU.bypass,
                replica_groups=[list(range(NCORES))],
                ins=[cc_in.opt()], outs=[cc_out.opt()])
            inc(ic, "cc", 1)
            wait(G, "cc")
            gk = gdma("sp", out=gath[:, :, :],
                      in_=cc_out.rearrange("(r p) k -> p r k", p=128))
            load_weights_upto(17)

            def phaseB_head(pi, tiles, b0, b1):
                ym = emit_y_pass(tiles, gcol=C2_LNG, bcol=C2_LNB,
                                 mask255=(pi == 0), restore_lag=(pi > 0))
                rb0 = 255 if pi == 0 else b0
                rm = emit_mix(FA, tiles, rb0, b1, cpk, CP_TMR, CP_MR1, fa_war)
                fd_war[0] = ("vv", rm)
                return rm, rb0

            rm0, rb0_0 = phaseB_head(0, Y_PASSES[0], *MIX_BOUNDS[0])
            V.memset(fmX[:, :, SW:SW + 704], 0.0)
            V.drain()
            srw_init = inc(V.nop(), "vv", 1)

            V.wait_ge(sems["sp"], gk)
            wait(V, "ldc")
            for c in range(8):
                V.tensor_scalar(stats[:, PO + 8 * c:PO + 8 * c + 8],
                                cpk2[:, C2_KAP:C2_KAP + 8],
                                cpk[:, CP_W + c:CP_W + c + 1], None, ALU.mult)
            V.drain()
            vq = inc(V.nop(), "vv", 1)
            wait(S, "vv", vq)
            S.activation(stats[:, PO:PO + 64], stats[:, PO:PO + 64], AF.Exp)
            S.drain()
            eq = inc(S.nop(), "ev", 1)
            wait(V, "ev", eq)
            for c in range(8):
                sl8 = stats[:, PO + 8 * c:PO + 8 * c + 8]
                V.tensor_tensor(sl8, sl8, cpk2[:, C2_KIN:C2_KIN + 8],
                                ALU.mult)
                V.drain()
                V.tensor_tensor(stats[:, ST_S3:ST_S3 + 8], sl8,
                                gath[:, :, c], ALU.mult)
                V.drain()
                V.reduce_sum(carry[:, CA_NIN + c:CA_NIN + c + 1],
                             stats[:, ST_S3:ST_S3 + 8], AXX)
                V.drain()
                V.tensor_tensor(stats[:, ST_S3:ST_S3 + 8], sl8,
                                gath[:, :, 8 + c], ALU.mult)
                V.drain()
                V.reduce_sum(carry[:, CA_DIN + c:CA_DIN + c + 1],
                             stats[:, ST_S3:ST_S3 + 8], AXX)
                V.drain()
            V.tensor_scalar(carry[:, CA_T0:CA_T0 + 8],
                            carry[:, CA_DIN:CA_DIN + 8], WEPS, None, ALU.add)
            V.drain()
            V.reciprocal(carry[:, CA_T0:CA_T0 + 8], carry[:, CA_T0:CA_T0 + 8])
            V.drain()
            V.tensor_tensor(carry[:, CA_T0:CA_T0 + 8],
                            carry[:, CA_T0:CA_T0 + 8],
                            carry[:, CA_NIN:CA_NIN + 8], ALU.mult)
            V.drain()
            inc(V.nop(), "vv", 1)

            wait_weight(V, "l1wo")
            _, wob = wuse("l1wo")
            wait(V, "mm")
            V.tensor_scalar(wob[:, :, :], wob[:, :, :], 0.5, None, ALU.mult)
            V.drain()
            wo_scaled = inc(V.nop(), "vv", 1)

            reload_mark = [0]
            rm_cur, rb0_cur = rm0, rb0_0
            head_next = [None]
            for pi, tiles in enumerate(Y_PASSES):
                b0, b1 = MIX_BOUNDS[pi]
                rm, rb0 = rm_cur, rb0_cur
                yb = 128 * tiles[0]
                wt = tiles
                wt_base = 128 * wt[0]
                lastp = (pi == len(Y_PASSES) - 1)
                wait(T, "vv", rm)
                wait_weight(T, "l1wr")
                _, wrb = wuse("l1wr")
                rcols = b1 - rb0
                r_mm = 0
                c_mark = cnt["vv"]
                for c in range(8):
                    drain_thunks(1)
                    wait(S, "vv", c_mark)
                    rlast = None
                    for (n0, w) in nslices(rcols):
                        bi, ps = take_bank()
                        for k in range(8):
                            rlast = T.matmul(ps[:, 0:w],
                                             wrb[:, k, 128 * c:128 * c + 128],
                                             fmX[:, k, FA + n0:FA + n0 + w],
                                             start=(k == 0), stop=(k == 7))
                        mk = inc(rlast, "mm", 1)
                        wait(S, "mm", mk)
                        S.activation(scr[:, 3, n0:n0 + w], ps[:, 0:w], AF.Tanh,
                                     scale=0.5)
                        S.drain()
                        bmark[bi] = ("ev", inc(S.nop(), "ev", 1))
                    r_mm = cnt["mm"]
                    if lastp and c == 7:
                        weight_done("l1wr")
                    o0 = b0 - 256
                    ocols = b1 - b0
                    G.wait_ge(sems["vv"], c_mark)
                    gdma("sp", out=scr[:, 0, 0:ocols],
                         in_=spill_h[c, 0, :, o0:o0 + ocols])
                    rlk = gdma("sp", out=scr[:, 1, 0:ocols],
                               in_=spill_h[c, 1, :, o0:o0 + ocols])
                    S.wait_ge(sems["sp"], rlk)
                    S.wait_ge(sems["vv"], c_mark)
                    S.activation(scr[:, 2, 0:ocols], rampb[:, o0:o0 + ocols],
                                 AF.Exp, scale=cpk[:, CP_W + c:CP_W + c + 1])
                    S.drain()
                    dk = inc(S.nop(), "ev", 1)
                    wait(V, "ev", dk)
                    V.scalar_tensor_tensor(scr[:, 0, 0:ocols],
                                           scr[:, 2, 0:ocols],
                                           carry[:, CA_NIN + c:CA_NIN + c + 1],
                                           scr[:, 0, 0:ocols],
                                           ALU.mult, ALU.add)
                    V.drain()
                    V.scalar_tensor_tensor(scr[:, 1, 0:ocols],
                                           scr[:, 2, 0:ocols],
                                           carry[:, CA_DIN + c:CA_DIN + c + 1],
                                           scr[:, 1, 0:ocols],
                                           ALU.mult, ALU.add)
                    V.drain()
                    V.tensor_scalar(scr[:, 1, 0:ocols], scr[:, 1, 0:ocols],
                                    WEPS, None, ALU.add)
                    V.drain()
                    V.reciprocal(scr[:, 1, 0:ocols], scr[:, 1, 0:ocols])
                    V.drain()
                    V.tensor_tensor(scr[:, 2, 0:ocols], scr[:, 0, 0:ocols],
                                    scr[:, 1, 0:ocols], ALU.mult)
                    V.drain()
                    toff = b0 - rb0
                    if pi == 0:
                        V.wait_ge(sems["vv"], srw_init)
                    V.scalar_tensor_tensor(
                        fmX[:, c, SW + (b0 - wt_base):
                            SW + (b0 - wt_base) + ocols],
                        scr[:, 3, toff:toff + ocols], 1.0,
                        scr[:, 2, 0:ocols], ALU.add, ALU.mult)
                    V.drain()
                    if pi == 0:
                        V.scalar_tensor_tensor(
                            fmX[:, c, SW + 127:SW + 128],
                            scr[:, 3, 0:1], 1.0,
                            carry[:, CA_T0 + c:CA_T0 + c + 1],
                            ALU.add, ALU.mult)
                        V.drain()
                    c_mark = inc(V.nop(), "vv", 1)
                fa_war[0] = ("mm", r_mm)
                srw_done = c_mark
                if not lastp:
                    rm_cur, rb0_cur = phaseB_head(pi + 1, Y_PASSES[pi + 1],
                                                  *MIX_BOUNDS[pi + 1])
                drain_thunks(len(defq))
                wait(T, "vv", max(srw_done, wo_scaled))

                def evac_po(m, n0, w, ps, mk):
                    wait(S, "mm", mk)
                    for wv in fb_war[0]:
                        war_wait(S, wv)
                    S.activation(fmB[:, m, n0:n0 + w], ps[:, 0:w], AF.Identity)
                    S.drain()
                    return ("ev", inc(S.nop(), "ev", 1))

                om = proj("l1wo", fmX, SW, 128 * len(wt), list(range(8)),
                          evac_po, last_use=lastp, drain=True)
                postln_deferred(wt, fmB, 0, om[max(om)], fold=True, eng="V",
                                mini_ln=True, war_out=fb_war)

            dump("x3", stream[:, :, :])

        if stage >= 4:
            wait(G, "vv"); wait(G, "ev"); wait(G, "mm")
            relu_sc = [scr[:, 2, 0:512], scr[:, 3, 0:512]]
            sq_mark = [cnt["vv"], cnt["vv"]]
            sw_war = [("mm", cnt["mm"])]
            f1_post = [[2, 3, 4, 5], [6, 7, 8, 9, 10],
                       [11, 12, 13, 14, 15], [16, 17]]
            for pi, tiles in enumerate(Y_PASSES):
                b0, b1 = MIX_BOUNDS[pi]
                ncols = b1 - b0
                lastp = (pi == len(Y_PASSES) - 1)
                load_weights_upto(18 + 3 * pi)
                ym = emit_y_pass(tiles, mask255=(pi == 0),
                                 restore_lag=(pi > 0))
                rm = emit_mix(FA, tiles, b0, b1, cpk2, C2_MUR, C2_MUR1, fa_war)
                km = emit_mix(SW, tiles, b0, b1, cpk2, C2_MUK, C2_MUK1, sw_war)
                fd_war[0] = ("vv", km)
                wait(T, "vv", rm)

                def evac_r2(m, n0, w, ps, mk):
                    wait(S, "mm", mk)
                    S.wait_ge(sems["vv"], tr2_war[0])
                    S.activation(tr2[:, m, n0:n0 + w], ps[:, 0:w], AF.Tanh,
                                 scale=0.5)
                    S.drain()
                    return ("ev", inc(S.nop(), "ev", 1))

                proj("fwr", fmX, FA, ncols, list(range(8)), evac_r2,
                     last_use=True, drain=True)
                fa_war[0] = ("mm", cnt["mm"])
                load_weights_upto(19 + 3 * pi)
                load_weights_upto(20 + 3 * pi)
                wait_weight(V, "fwv")
                _, wvb2 = wuse("fwv")
                wait(V, "mm")
                V.tensor_scalar(wvb2[:, :, :], wvb2[:, :, :], 0.5, None,
                                ALU.mult)
                V.drain()
                fwv_scaled = inc(V.nop(), "vv", 1)
                wait(T, "vv", km)

                ecnt = [0]

                def evac_relu_sq(m, n0, w, ps, mk):
                    par = ecnt[0] % 2
                    ecnt[0] += 1
                    wait(S, "mm", mk)
                    S.wait_ge(sems["vv"], sq_mark[par])
                    S.activation(relu_sc[par][:, 0:w], ps[:, 0:w], AF.Relu)
                    S.drain()
                    ek = inc(S.nop(), "ev", 1)
                    wait(V, "ev", ek)
                    V.tensor_tensor(fmC[:, m, n0:n0 + w],
                                    relu_sc[par][:, 0:w],
                                    relu_sc[par][:, 0:w], ALU.mult)
                    V.drain()
                    sq_mark[par] = inc(V.nop(), "vv", 1)
                    return ("vv", sq_mark[par])

                kpm = proj("fwk", fmX, SW, ncols, list(range(8)),
                           evac_relu_sq, last_use=True, drain=True)
                sw_war[0] = ("mm", cnt["mm"])
                wait(T, "vv", max(cnt["vv"], fwv_scaled))

                drain_thunks(len(defq))

                def evac_out(m, n0, w, ps, mk):
                    wait(V, "mm", mk)
                    for wv in fb_war[0]:
                        war_wait(V, wv)
                    V.scalar_tensor_tensor(fmB[:, m, n0:n0 + w],
                                           tr2[:, m, n0:n0 + w], 1.0,
                                           ps[:, 0:w], ALU.add, ALU.mult)
                    V.drain()
                    return ("vv", inc(V.nop(), "vv", 1))

                vm_ = proj("fwv", fmC, 0, ncols, list(range(8)), evac_out,
                           last_use=True,
                           slice_wait=lambda n0, w: [kpm[n0]])
                tr2_war[0] = vm_[max(vm_)][1]
                pt = f1_post[pi]
                postln_deferred(pt, fmB, 128 * pt[0] - b0,
                                vm_[max(vm_)], fold=False, eng="V",
                                war_out=fb_war)

        drain_thunks(len(defq))
        wait(SY, "vv"); wait(SY, "ev"); wait(SY, "pp")
        iout = SY.dma_start(out=out_h.rearrange("(t p) d -> p t d", p=128),
                            in_=stream[:, 2:18, :])
        inc(iout, "st", 16)
        wait(SY, "st")
        if stage >= 3:
            arena2.free()
        else:
            arenaF.free()

    stack.close()
    return nc, dbg_h


def _fm_vec(v):
    return np.asarray(v, np.float32).reshape(8, 128).T.copy()


def _prep_core_inputs(core, full):
    b, s = core // 4, core % 4
    T0 = 2048 * s
    f32 = np.float32
    bf16 = ml_dtypes.bfloat16
    x = np.asarray(full["L0_attn_x"][b], f32)
    x0 = np.zeros((EXT, D), f32)
    lo, hi = T0 - 256, T0 + 2176
    slo, shi = max(lo, 0), min(hi, 8192)
    x0[slo - lo:shi - lo] = x[slo:shi]
    kvmask = np.zeros((EXT,), f32)
    kvmask[slo - lo:shi - lo] = 1.0
    kvm_small = np.concatenate([kvmask[0:512], kvmask[2048:2432]])
    kk, qq = np.meshgrid(np.arange(128), np.arange(128), indexing="ij")
    m0 = (kk < qq).astype(f32) * -90.0
    m2 = (kk >= qq).astype(f32) * -90.0
    bmask = np.zeros((128, 2, 512), f32)
    for i in range(4):
        bmask[:, 0, 128 * i:128 * (i + 1)] = m0
        bmask[:, 1, 128 * i:128 * (i + 1)] = m2
    ramp = np.tile(np.arange(1, TC + 1, dtype=f32)[None, :], (128, 1))

    cp = np.zeros((128, 96), f32)
    cp[:, CP_BQ:CP_BQ + 8] = _fm_vec(full["L0_attn_bq"])
    bk = np.asarray(full["L0_attn_bk"], f32)
    bkd = np.zeros((128, 2), f32)
    for g in range(2):
        bkd[0:64, g] = bk[64 * g:64 * g + 64]
        bkd[64:128, g] = bk[64 * g:64 * g + 64]
    cp[:, CP_BKD:CP_BKD + 2] = bkd
    cp[:, CP_BV:CP_BV + 1] = np.asarray(full["L0_attn_bv"], f32)[:, None]
    cp[:, CP_BO:CP_BO + 8] = _fm_vec(full["L0_attn_bo"])
    cp[:, CP_BF:CP_BF + 8] = _fm_vec(full["L0_ffn_bias"])
    cp[:, CP_W:CP_W + 8] = _fm_vec(full["L1_attn_time_decay"])
    tmk = _fm_vec(np.asarray(full["L1_attn_time_mix_k"]).reshape(-1))
    tmv = _fm_vec(np.asarray(full["L1_attn_time_mix_v"]).reshape(-1))
    tmr = _fm_vec(np.asarray(full["L1_attn_time_mix_r"]).reshape(-1))
    cp[:, CP_TMK:CP_TMK + 8] = tmk
    cp[:, CP_TMV:CP_TMV + 8] = tmv
    cp[:, CP_TMR:CP_TMR + 8] = tmr
    cp[:, CP_MK1:CP_MK1 + 8] = 1.0 - tmk
    cp[:, CP_MV1:CP_MV1 + 8] = 1.0 - tmv
    cp[:, CP_MR1:CP_MR1 + 8] = 1.0 - tmr
    cp[:, CP_FM] = 0.0 if s == 0 else 1.0
    cp[:, CP_EPS] = EPS
    cp[:, CP_HALF] = 0.5
    cp2 = np.zeros((128, 64), f32)
    cp2[:, C2_LNG:C2_LNG + 8] = _fm_vec(full["L1_attn_ln_g"])
    cp2[:, C2_LNB:C2_LNB + 8] = _fm_vec(full["L1_attn_ln_b"])
    mur = _fm_vec(full["L1_ffn_mu_r"])
    muk = _fm_vec(full["L1_ffn_mu_k"])
    cp2[:, C2_MUR:C2_MUR + 8] = mur
    cp2[:, C2_MUK:C2_MUK + 8] = muk
    cp2[:, C2_MUR1:C2_MUR1 + 8] = 1.0 - mur
    cp2[:, C2_MUK1:C2_MUK1 + 8] = 1.0 - muk
    kap = np.zeros((8,), f32)
    kin = np.zeros((8,), f32)
    for r in range(8):
        if r // 4 == b and (r % 4) < s:
            kin[r] = 1.0
            kap[r] = 2048.0 * (s - 1 - (r % 4))
    cp2[:, C2_KAP:C2_KAP + 8] = np.tile(kap[None, :], (128, 1))
    cp2[:, C2_KIN:C2_KIN + 8] = np.tile(kin[None, :], (128, 1))

    wk = np.asarray(full["L0_attn_Wk"], f32)
    wkd = np.concatenate([wk[:, 0:64], wk[:, 0:64], wk[:, 64:128],
                          wk[:, 64:128]], axis=1)
    wkv = np.concatenate([wkd, np.asarray(full["L0_attn_Wv"], f32)], axis=1)
    perm = []
    for cc in range(8):
        g, i = cc // 4, cc % 4
        a, bb = 8 * g + 2 * i, 8 * g + 2 * i + 1
        perm.extend(range(64 * a, 64 * a + 64))
        perm.extend(range(64 * bb, 64 * bb + 64))
    wo = np.ascontiguousarray(np.asarray(full["L0_attn_Wo"], f32)[perm, :])

    tT = lambda w: np.ascontiguousarray(np.asarray(w, f32).T.astype(bf16))
    wbf = lambda w: np.asarray(w, f32).astype(bf16)
    return {
        "x0": x0,
        "wq": wbf(full["L0_attn_Wq"]), "wkv": wkv.astype(bf16),
        "wo": wo.astype(bf16),
        "wf": wbf(full["L0_ffn_weight"]),
        "wa1": wbf(full["L0_ffn_aux_weight1"]),
        "wa2": wbf(full["L0_ffn_aux_weight2"]),
        "l1wk": tT(full["L1_attn_Wk"]), "l1wv": tT(full["L1_attn_Wv"]),
        "l1wr": tT(full["L1_attn_Wr"]), "l1wo": tT(full["L1_attn_Wo"]),
        "fwr": tT(full["L1_ffn_W_r"]), "fwk": tT(full["L1_ffn_W_k"]),
        "fwv": tT(full["L1_ffn_W_v"]),
        "cpack": cp, "cpack2": cp2,
        "kvmask": np.tile(kvm_small[None, :], (128, 1)).astype(bf16),
        "identbf": np.eye(128, dtype=f32).astype(bf16),
        "bmask": bmask.astype(bf16),
        "rampbf": ramp.astype(bf16),
    }


def kernel(**inputs):
    nc, _ = build()
    in_maps = [_prep_core_inputs(c, inputs) for c in range(NCORES)]
    res = run_bass_kernel_spmd(nc, in_maps, core_ids=list(range(NCORES)))
    outs = []
    for b in range(2):
        outs.append(np.concatenate(
            [res.results[4 * b + s]["out"] for s in range(4)], axis=0))
    return np.stack(outs, axis=0)
